# version 11
# baseline (speedup 1.0000x reference)
import contextlib
import numpy as np
import ml_dtypes
import concourse.bass as bass
import concourse.mybir as mybir
from concourse.bass_utils import run_bass_kernel_spmd

F32 = mybir.dt.float32
BF16 = mybir.dt.bfloat16
AF = mybir.ActivationFunctionType
OP = mybir.AluOpType

SEQ = 2048
DM = 1024
DC = 8
HID = 4096
HC = 32
NTILE = 8
NCH = 32
EPS = 1e-5


class Sem:
    def __init__(self, h):
        self.h = h
        self.n = 0


def _build(debug=False, ntile=NTILE, ncores=8, p3sub=7):
    nc = bass.Bass("TRN2", target_bir_lowering=False)

    x_d = nc.dram_tensor("x", [2, SEQ, DM], F32, kind="ExternalInput").ap()
    at_d = nc.dram_tensor("at", [128, 128], F32, kind="ExternalInput").ap()
    bt_d = nc.dram_tensor("bt", [DM, 128], F32, kind="ExternalInput").ap()
    ct_d = nc.dram_tensor("ct", [128, DM], F32, kind="ExternalInput").ap()
    drow_d = nc.dram_tensor("drow", [1, DM], F32, kind="ExternalInput").ap()
    grow_d = nc.dram_tensor("grow", [1, DM], F32, kind="ExternalInput").ap()
    brow_d = nc.dram_tensor("brow", [1, DM], F32, kind="ExternalInput").ap()
    w1t_d = nc.dram_tensor("w1t", [32, 128, 1024], F32, kind="ExternalInput").ap()
    w2t_d = nc.dram_tensor("w2t", [32, 128, 1024], F32, kind="ExternalInput").ap()
    h1t_d = nc.dram_tensor("h1t", [32, 128, 4096], F32, kind="ExternalInput").ap()
    h2t_d = nc.dram_tensor("h2t", [32, 128, 4096], F32, kind="ExternalInput").ap()
    w3t_d = nc.dram_tensor("w3t", [8, 128, 4096], F32, kind="ExternalInput").ap()
    s1_d = nc.dram_tensor("s1pk", [128, 32], F32, kind="ExternalInput").ap()
    s2_d = nc.dram_tensor("s2pk", [128, 32], F32, kind="ExternalInput").ap()
    h1s_d = nc.dram_tensor("h1spk", [128, 32], F32, kind="ExternalInput").ap()
    h2s_d = nc.dram_tensor("h2spk", [128, 32], F32, kind="ExternalInput").ap()
    w3s_d = nc.dram_tensor("w3spk", [128, 8], F32, kind="ExternalInput").ap()
    eye_d = nc.dram_tensor("eye", [128, 128], F32, kind="ExternalInput").ap()
    ones_d = nc.dram_tensor("ones", [1, 128], F32, kind="ExternalInput").ap()
    out_d = nc.dram_tensor("out", [2, SEQ, DM], F32, kind="ExternalOutput").ap()

    w1b_d = nc.dram_tensor("w1b", [32, 128, 1024], BF16, kind="Internal").ap()
    w2b_d = nc.dram_tensor("w2b", [32, 128, 1024], BF16, kind="Internal").ap()
    h1b_d = nc.dram_tensor("h1b", [32, 128, 4096], BF16, kind="Internal").ap()
    h2b_d = nc.dram_tensor("h2b", [32, 128, 4096], BF16, kind="Internal").ap()
    w3b_d = nc.dram_tensor("w3b", [8, 128, 4096], BF16, kind="Internal").ap()

    if debug:
        dbg_xb = nc.dram_tensor("dbg_xb", [128, 4352], F32, kind="ExternalOutput").ap()
        dbg_st = nc.dram_tensor("dbg_st", [128, 4352], BF16, kind="ExternalOutput").ap()
        dbg_mv = nc.dram_tensor("dbg_mv", [128, 64], F32, kind="ExternalOutput").ap()
        dbg_rs = nc.dram_tensor("dbg_rs", [128, 32], F32, kind="ExternalOutput").ap()
        dbg_myv = nc.dram_tensor("dbg_myv", [128, 64], F32, kind="ExternalOutput").ap()
        dbg_rsy = nc.dram_tensor("dbg_rsy", [128, 32], F32, kind="ExternalOutput").ap()

    ENGS = ("sync", "tensor", "vector", "scalar", "gpsimd")
    ST1 = {e: [] for e in ENGS}
    ST2 = {e: [] for e in ENGS}
    CUR = [ST1]
    WAITED = {e: {} for e in ENGS}
    GUARD = {}

    def em(eng, fn):
        CUR[0][eng].append(fn)

    def wait(eng, sem, val):
        if val <= 0:
            return
        if WAITED[eng].get(sem, 0) >= val:
            return
        WAITED[eng][sem] = val
        em(eng, lambda e, h=sem.h, v=val: e.wait_ge(h, v))

    def op(eng, fn, sem=None, k=1):
        if sem is not None:
            sem.n += k
            em(eng, lambda e, f=fn, h=sem.h, kk=k: f(e).then_inc(h, kk))
            return sem.n
        em(eng, fn)
        return None

    def dop(eng, fn, sem):
        wait(eng, sem, sem.n)
        return op(eng, fn, sem, 16)

    def gwait(eng, key):
        for s, v in GUARD.get(key, ()):
            wait(eng, s, v)

    def gset(key, *pairs):
        GUARD[key] = list(pairs)

    def materialize(blk, streams):
        @blk.sync
        def _(e):
            for f in streams["sync"]:
                f(e)

        @blk.tensor
        def _(e):
            for f in streams["tensor"]:
                f(e)

        @blk.vector
        def _(e):
            for f in streams["vector"]:
                f(e)

        @blk.scalar
        def _(e):
            for f in streams["scalar"]:
                f(e)

        @blk.gpsimd
        def _(e):
            for f in streams["gpsimd"]:
                f(e)

    es = contextlib.ExitStack()
    with es:
        names = ("cld", "cv0", "cmm", "x0", "v0", "sc0", "v0b", "x1",
                 "v1", "mmT", "v2", "mmB", "xb", "ac", "smm",
                 "cl", "clB", "clC", "clD", "cv", "cs", "csB", "csC", "csD", "x2", "mmC", "v5g", "v5",
                 "sc2", "v5b", "wl", "wlB", "wh", "mmL", "scL", "vL",
                 "mmH", "scH", "mmW", "vW", "mmC2", "vC2", "mmT3",
                 "v3", "x3", "x4", "mmF", "v7", "vout", "sout", "sdbg", "vsc", "p7")
        SM = {n: Sem(es.enter_context(nc.semaphore(n))) for n in names}
        (cld, cv0, cmm, x0, v0, sc0, v0b, x1, v1, mmT, v2, mmB, xb, ac,
         smm, cl, clB, clC, clD, cv, cs, csB, csC, csD, x2, mmC, v5g, v5,
         sc2, v5b, wl, wlB, wh, mmL, scL, vL, mmH, scH, mmW, vW, mmC2,
         vC2, mmT3, v3, x3, x4, mmF, v7, vout, sout, sdbg, vsc, p7) = (
            SM[n] for n in names)

        sbT = lambda n, shp, dt: es.enter_context(nc.sbuf_tensor(n, shp, dt))
        states = sbT("states", [128, 4352], BF16)
        xcol = lambda b, t: b * 2176 + 64 + t
        xn_tm = sbT("xn_tm", [128, 2 * DM], F32)
        xrb = sbT("xrb", [128, 2 * DM], F32)
        t1b = sbT("t1b", [128, DM], F32)
        t2b = sbT("t2b", [128, DM], F32)
        d1b = sbT("d1b", [128, DM], F32)
        gbc = sbT("gbc", [128, DM], F32)
        bbc = sbT("bbc", [128, DM], F32)
        ctb = sbT("ctb", [128, DM], BF16)
        eye_s = sbT("eye_s", [128, 128], F32)
        rowbuf = sbT("rowbuf", [1, DM], F32)
        ones_s = sbT("ones_s", [1, 128], F32)
        eps_s = sbT("eps_s", [128, 1], F32)
        s1pk = sbT("s1pk_s", [128, 32], F32)
        s2pk = sbT("s2pk_s", [128, 32], F32)
        h1spk = sbT("h1spk_s", [128, 32], F32)
        h2spk = sbT("h2spk_s", [128, 32], F32)
        w3spk = sbT("w3spk_s", [128, 8], F32)
        s1h = sbT("s1h", [128, 32], F32)
        s1h2 = sbT("s1h2", [128, 32], F32)
        muvar = sbT("muvar", [128, 64], F32)
        rstd = sbT("rstd", [128, 32], F32)
        sqtmp = sbT("sqtmp", [128, 32], F32)
        muyvar = sbT("muyvar", [128, 64], F32)
        rstdy = sbT("rstdy", [128, 32], F32)
        bnb = sbT("bnb", [128, 12], F32)

        psT = lambda n: es.enter_context(nc.psum_tensor(n, [128, 512], F32))
        pA, pA2, pB, pB2, pE, pE2, pG, pH = (psT(n) for n in
            ("pA", "pA2", "pB", "pB2", "pE", "pE2", "pG", "pH"))

        ph1 = contextlib.ExitStack()
        sb1 = lambda n, shp, dt: ph1.enter_context(nc.sbuf_tensor(n, shp, dt))
        xB = sb1("xB", [128, 4352], F32)
        sctmp = sb1("sctmp", [128, 16], F32)
        atb = sb1("atb", [128, 128], BF16)
        btb = sb1("btb", [128, DM], BF16)
        xnbw = sb1("xnbw", [128, DC * 128], BF16)
        cin = sb1("cin", [128, 2304], F32)
        cstmp = sb1("cstmp", [128, DM], BF16)

        dop("sync", lambda e: e.dma_start(cin[:, 0:128], at_d), cld)
        bt3 = bt_d.rearrange("(k p) n -> p k n", p=128)
        cin_bt = cin[:, 128:1152].rearrange("p (k n) -> p k n", n=128)
        dop("sync", lambda e: e.dma_start(cin_bt, bt3), cld)
        dop("sync", lambda e: e.dma_start(cin[:, 1152:2176], ct_d), cld)
        dop("sync", lambda e: e.dma_start(eye_s[:], eye_d), cld)
        dop("sync", lambda e: e.dma_start(ones_s[:], ones_d), cld)
        for dd, ss in ((s1_d, s1pk), (s2_d, s2pk), (h1s_d, h1spk),
                       (h2s_d, h2spk), (w3s_d, w3spk)):
            dop("sync", lambda e, d=dd, s=ss: e.dma_start(s[:], d), cld)
        wait("vector", cld, 160)
        op("vector", lambda e: e.memset(eps_s[:], EPS), cv0)
        op("vector", lambda e: e.memset(xB[:], 0.0), cv0)
        op("vector", lambda e: e.memset(states[:], 0.0), cv0)
        op("vector", lambda e: e.tensor_copy(atb[:], cin[:, 0:128]), cv0)
        op("vector", lambda e: e.tensor_copy(btb[:], cin[:, 128:1152]), cv0)
        op("vector", lambda e: e.tensor_copy(ctb[:], cin[:, 1152:2176]), cv0)
        op("vector", lambda e: e.tensor_scalar(
            s1h[:], s1pk[:], 0.5, None, op0=OP.mult), cv0)
        wait("vector", cv0, cv0.n)
        op("vector", lambda e: e.tensor_tensor(
            s1h2[:], s1h[:], s2pk[:], op=OP.mult), cv0)
        CV0_CONSTS = cv0.n

        for rd, dst, plus1 in ((drow_d, d1b, True), (grow_d, gbc, False),
                               (brow_d, bbc, False)):
            gwait("sync", "rowbuf")
            ldn = dop("sync", lambda e, d=rd: e.dma_start(rowbuf[:], d), cld)
            for hh in range(2):
                wait("tensor", cld, ldn)
                gwait("tensor", "pG")
                mn = op("tensor", lambda e, h=hh: e.matmul(
                    pG[:, 0:512], ones_s[0:1, :],
                    rowbuf[0:1, h * 512:(h + 1) * 512],
                    start=True, stop=True), cmm)
                wait("vector", cmm, mn)
                if plus1:
                    en = op("vector", lambda e, h=hh, ds=dst: e.tensor_scalar(
                        ds[:, h * 512:(h + 1) * 512], pG[:, 0:512],
                        1.0, None, op0=OP.add), cv0)
                else:
                    en = op("vector", lambda e, h=hh, ds=dst: e.tensor_copy(
                        ds[:, h * 512:(h + 1) * 512], pG[:, 0:512]), cv0)
                gset("pG", (cv0, en))
            gset("rowbuf", (cmm, cmm.n))
        CV0_ALL = cv0.n

        for i in range(32):
            j, b = i // 2, i % 2
            c = b * 16 + j
            slot = i % 2
            gwait("gpsimd", ("xrb", slot))
            ldn = dop("gpsimd", lambda e, bb_=b, r0=j * 128, sl=slot: e.dma_start(
                xrb[:, sl * DM:(sl + 1) * DM], x_d[bb_, r0:r0 + 128, :]), x0)
            wait("vector", x0, ldn)
            gwait("vector", "bnb")
            op("vector", lambda e, sl=slot: e.bn_stats(
                bnb[:, 0:6], xrb[:, sl * DM:sl * DM + 512]), v0)
            sn2 = op("vector", lambda e, sl=slot: e.bn_stats(
                bnb[:, 6:12], xrb[:, sl * DM + 512:(sl + 1) * DM]), v0)
            wait("vector", v0, sn2)
            an = op("vector", lambda e, cc=c: e.bn_aggr(
                muvar[:, 2 * cc:2 * cc + 2], bnb[:]), v0)
            gset(("xrb", slot), (v0, sn2))
            gset("bnb", (v0, an))

        wait("scalar", v0, v0.n)
        wait("scalar", cv0, 1)
        qn = op("scalar", lambda e: e.activation(
            sqtmp[:], muvar[:, 1:64:2], AF.Sqrt, bias=eps_s[:]), sc0)
        wait("vector", sc0, qn)
        op("vector", lambda e: e.reciprocal(rstd[:], sqtmp[:]), v0b)
        gset("sqtmp", (v0b, v0b.n))

        wait("tensor", cld, 160)
        wait("tensor", cv0, CV0_CONSTS)
        wait("vector", v0, 96)
        for i in range(32):
            j, b = i // 2, i % 2
            c = b * 16 + j
            slot = i % 2
            gwait("gpsimd", ("xrb", slot))
            ldn = dop("gpsimd", lambda e, bb_=b, r0=j * 128, sl=slot: e.dma_start(
                xrb[:, sl * DM:(sl + 1) * DM], x_d[bb_, r0:r0 + 128, :]), x1)
            wait("vector", x1, ldn)
            wait("vector", v0b, 1)
            gwait("vector", ("xn_tm", slot))
            tn = op("vector", lambda e, sl=slot, cc=c: e.tensor_scalar(
                xn_tm[:, sl * DM:(sl + 1) * DM], xrb[:, sl * DM:(sl + 1) * DM],
                muvar[:, 2 * cc:2 * cc + 1], rstd[:, cc:cc + 1],
                op0=OP.subtract, op1=OP.mult), v1)
            gset(("xrb", slot), (v1, tn))
            wait("tensor", v1, tn)
            gwait("tensor", "pA01")
            for k in range(8):
                bank = pA if k < 4 else pA2
                cc0 = (k % 4) * 128
                op("tensor", lambda e, bk=bank, c0=cc0, sl=slot, kk=k: e.transpose(
                    bk[:, c0:c0 + 128],
                    xn_tm[:, sl * DM + kk * 128:sl * DM + (kk + 1) * 128],
                    eye_s[:]), mmT if k == 7 else None)
            gset(("xn_tm", slot), (mmT, mmT.n))
            wait("vector", mmT, mmT.n)
            gwait("vector", "xnbw")
            for k in range(8):
                bank = pA if k < 4 else pA2
                cc0 = (k % 4) * 128
                op("vector", lambda e, bk=bank, c0=cc0, kk=k: e.tensor_copy(
                    xnbw[:, kk * 128:(kk + 1) * 128], bk[:, c0:c0 + 128]),
                    v2 if k == 7 else None)
            gset("pA01", (v2, v2.n))
            wait("tensor", v2, v2.n)
            gwait("tensor", "pB0")
            for k in range(8):
                op("tensor", lambda e, kk=k: e.matmul(
                    pB[:, 0:128], btb[:, kk * 128:(kk + 1) * 128],
                    xnbw[:, kk * 128:(kk + 1) * 128],
                    start=(kk == 0), stop=(kk == 7)), mmB if k == 7 else None)
            gset("xnbw", (mmB, mmB.n))
            wait("vector", mmB, mmB.n)
            op("vector", lambda e, b_=b, jj=j: e.tensor_copy(
                xB[:, xcol(b_, jj * 128):xcol(b_, jj * 128) + 128], pB[:, 0:128]), xb)
            gset("pB0", (xb, xb.n))

        CSEM = {"w1": cl, "w2": clB, "h1": clC, "h2": clD, "w3": cv}
        for nm_, s_, d_ in (("w1", w1t_d, w1b_d), ("w2", w2t_d, w2b_d),
                            ("h1", h1t_d, h1b_d), ("h2", h2t_d, h2b_d),
                            ("w3", w3t_d, w3b_d)):
            op("gpsimd", lambda e, ss=s_, dd=d_: e.dma_start(dd, ss), CSEM[nm_], 16)

        def emit_x2(j):
            for b in (0, 1):
                slot = b
                gwait("scalar", ("xrb", slot))
                ldn = dop("scalar", lambda e, bb_=b, r0=j * 128, sl=slot: e.dma_start(
                    xrb[:, sl * DM:(sl + 1) * DM], x_d[bb_, r0:r0 + 128, :]), x2)
                X2N[(j, b)] = ldn

        def emit_p2(j):
            for b in (0, 1):
                c = b * 16 + j
                slot = b
                wait("tensor", ac, NBURN + (j % 4) * 128 + 128)
                gwait("tensor", "pG")
                gwait("tensor", "pH")
                sc0_ = xcol(b, (c % 16) * 128)
                op("tensor", lambda e, s0=sc0_: e.matmul(
                    pG[:, 0:512], states[:, s0:s0 + 128],
                    ctb[:, 0:512], start=True, stop=True), mmC)
                mn = op("tensor", lambda e, s0=sc0_: e.matmul(
                    pH[:, 0:512], states[:, s0:s0 + 128],
                    ctb[:, 512:1024], start=True, stop=True), mmC)
                wait("vector", mmC, mn)
                gwait("vector", "cstmp")
                op("vector", lambda e: e.tensor_copy(cstmp[:, 0:512], pG[:, 0:512]), v5g)
                gn = op("vector", lambda e: e.tensor_copy(
                    cstmp[:, 512:1024], pH[:, 0:512]), v5g)
                gset("pG", (v5g, gn))
                gset("pH", (v5g, gn))
                wait("vector", x2, X2N[(j, b)])
                wait("vector", cv0, CV0_ALL)
                gwait("vector", "t1t2")
                t1n = op("vector", lambda e, sl=slot, cc=c: e.tensor_scalar(
                    t1b[:], xrb[:, sl * DM:(sl + 1) * DM],
                    muvar[:, 2 * cc:2 * cc + 1], rstd[:, cc:cc + 1],
                    op0=OP.subtract, op1=OP.mult), v5)
                gset(("xrb", slot), (v5, t1n))
                wait("vector", v5, t1n)
                t2n = op("vector", lambda e: e.tensor_tensor(
                    t2b[:], t1b[:], d1b[:], op=OP.mult), v5)
                wait("vector", v5, t2n)
                wait("vector", v5g, gn)
                yn = op("vector", lambda e: e.tensor_tensor(
                    t1b[:], t2b[:], cstmp[:], op=OP.add), v5)
                gset("cstmp", (v5, yn))
                wait("vector", v5, yn)
                gwait("vector", "bnb")
                op("vector", lambda e: e.bn_stats(bnb[:, 0:6], t1b[:, 0:512]), v5)
                s2n = op("vector", lambda e: e.bn_stats(
                    bnb[:, 6:12], t1b[:, 512:1024]), v5)
                wait("vector", v5, s2n)
                an = op("vector", lambda e, cc=c: e.bn_aggr(
                    muyvar[:, 2 * cc:2 * cc + 2], bnb[:]), v5)
                gset("bnb", (v5, an))
                gset("t1t2", (v5, an))

        X2N = {}
        NBURN = 48
        st_h = states.tensor if hasattr(states, "tensor") else states
        xb_h = xB.tensor if hasattr(xB, "tensor") else xB
        def stv(off):
            return bass.AP(st_h, off, [[4352, 128], [2176, 2], [512, 4]])
        def xbv(off):
            return bass.AP(xb_h, off, [[4352, 128], [2176, 2], [512, 4]])
        wait("tensor", cv0, CV0_CONSTS)
        wait("vector", xb, 32)
        wait("scalar", xb, 32)
        for n in range(NBURN + 512):
            i = n - NBURN
            bank = pE if n % 2 == 0 else pE2
            s8 = (n % 2) * 8
            wait("tensor", ac, n)
            mn = op("tensor", lambda e, bk=bank, off=63 + i: e.matmul(
                bk[:, 0:8], atb[:], stv(off), start=True, stop=True), smm)
            wait("vector", smm, mn)
            an = op("vector", lambda e, bk=bank, ss=s8, off=64 + i: e.tensor_tensor(
                sctmp[:, ss:ss + 8], bk[:, 0:8], xbv(off), op=OP.add), vsc)
            wait("scalar", vsc, an)
            op("scalar", lambda e, ss=s8, off=64 + i: e.activation(
                stv(off), sctmp[:, ss:ss + 8], AF.Tanh), ac)
        gset("pE", (vsc, vsc.n))
        gset("pE2", (vsc, vsc.n))
        for j in range(16):
            emit_x2(j)
            emit_p2(j)
        gset("pE", (ac, ac.n))
        gset("pE2", (ac, ac.n))

        wait("scalar", v5, v5.n)
        gwait("scalar", "sqtmp")
        qn = op("scalar", lambda e: e.activation(
            sqtmp[:], muyvar[:, 1:64:2], AF.Sqrt, bias=eps_s[:]), sc2)
        wait("vector", sc2, qn)
        op("vector", lambda e: e.reciprocal(rstdy[:], sqtmp[:]), v5b)

        if debug:
            wait("gpsimd", xb, 32)
            dop("gpsimd", lambda e: e.dma_start(dbg_xb, xB[:]), sdbg)
            wait("gpsimd", ac, ac.n)
            dop("gpsimd", lambda e: e.dma_start(dbg_st, states[:]), sdbg)
            wait("gpsimd", v0b, 1)
            dop("gpsimd", lambda e: e.dma_start(dbg_mv, muvar[:]), sdbg)
            dop("gpsimd", lambda e: e.dma_start(dbg_rs, rstd[:]), sdbg)
            wait("gpsimd", v5b, 1)
            dop("gpsimd", lambda e: e.dma_start(dbg_myv, muyvar[:]), sdbg)
            dop("gpsimd", lambda e: e.dma_start(dbg_rsy, rstdy[:]), sdbg)

        with nc.Block("ph1") as blk1:
            materialize(blk1, ST1)
        ph1.close()

        CUR[0] = ST2
        sb2 = lambda n, shp, dt: es.enter_context(nc.sbuf_tensor(n, shp, dt))
        xnbt = sb2("xnbt", [128, DC * 512], BF16)
        hbuf = sb2("hbuf", [128, HC * 512], BF16)
        g1buf = sb2("g1buf", [128, HC * 512], BF16)
        hring = sb2("hring", [128, 2 * 4096], BF16)
        l1ring = sb2("l1ring", [128, 4 * 1024], BF16)
        ffn_fm = sb2("ffn_fm", [128, 8 * 512], F32)
        cs_tile = sb2("cs_tile", [128, 4 * 1024], BF16)
        xrb4 = sb2("xrb4", [128, 2 * DM], F32)
        out_sb = sb2("out_sb", [128, 2 * DM], F32)
        thbuf = sb2("thbuf", [128, 2 * 512], F32)
        ubuf = sb2("ubuf", [128, 512], F32)
        silubuf = sb2("silubuf", [128, 512], F32)

        hslot = lambda s: hring[:, s * 4096:(s + 1) * 4096]
        l1slot = lambda s: l1ring[:, s * 1024:(s + 1) * 1024]

        PH1_FINAL = [(s_, s_.n) for s_ in (
            cld, cv0, cmm, x0, v0, sc0, v0b, x1, v1, mmT, v2, mmB, xb,
            ac, smm, cl, clB, clC, clD, cv, cs, csB, csC, csD, x2, mmC,
            v5g, v5, sc2, v5b, vsc)]
        for eng in ("sync", "vector", "scalar", "gpsimd", "tensor"):
            for s_, v_ in PH1_FINAL:
                wait(eng, s_, v_)

        hring_i = [0]
        for tj in range(ntile):
            tb = tj // 4
            for cloc in range(4 if p3sub >= 1 else 0):
                c = 4 * tj + cloc
                r0 = (c % 16) * 128
                slot = c % 2
                gwait("gpsimd", ("xrb", slot))
                ldn = dop("gpsimd", lambda e, bb_=tb, rr=r0, sl=slot: e.dma_start(
                    xrb[:, sl * DM:(sl + 1) * DM], x_d[bb_, rr:rr + 128, :]), x3)
                wait("vector", x3, ldn)
                gwait("vector", ("xn_tm", slot))
                if cloc == 0:
                    gwait("vector", "xnbt")
                tn = op("vector", lambda e, sl=slot, cc=c: e.tensor_scalar(
                    xn_tm[:, sl * DM:(sl + 1) * DM], xrb[:, sl * DM:(sl + 1) * DM],
                    muvar[:, 2 * cc:2 * cc + 1], rstd[:, cc:cc + 1],
                    op0=OP.subtract, op1=OP.mult), v1)
                gset(("xrb", slot), (v1, tn))
                wait("tensor", v1, tn)
                gwait("tensor", "pA01")
                gwait("tensor", "pAL")
                gwait("tensor", "pA2L")
                for k in range(8):
                    bank = pA if k < 4 else pA2
                    cc0 = (k % 4) * 128
                    op("tensor", lambda e, bk=bank, c0=cc0, sl=slot, kk=k: e.transpose(
                        bk[:, c0:c0 + 128],
                        xn_tm[:, sl * DM + kk * 128:sl * DM + (kk + 1) * 128],
                        eye_s[:]), mmT3 if k == 7 else None)
                gset(("xn_tm", slot), (mmT3, mmT3.n))
                wait("vector", mmT3, mmT3.n)
                for k in range(8):
                    bank = pA if k < 4 else pA2
                    cc0 = (k % 4) * 128
                    op("vector", lambda e, bk=bank, c0=cc0, kk=k, cl_=cloc: e.tensor_copy(
                        xnbt[:, kk * 512 + cl_ * 128:kk * 512 + (cl_ + 1) * 128],
                        bk[:, c0:c0 + 128]), v3 if k == 7 else None)
                gset("pA01", (v3, v3.n))
            for cloc in range(4 if p3sub >= 2 else 0):
                c = 4 * tj + cloc
                gwait("tensor", "pE")
                gwait("tensor", "pE2")
                if cloc == 0:
                    gwait("tensor", "cs_tile_w")
                sc0_ = xcol(c // 16, (c % 16) * 128)
                op("tensor", lambda e, s0=sc0_: e.matmul(
                    pE[:, 0:512], states[:, s0:s0 + 128],
                    ctb[:, 0:512], start=True, stop=True), mmC2)
                mn = op("tensor", lambda e, s0=sc0_: e.matmul(
                    pE2[:, 0:512], states[:, s0:s0 + 128],
                    ctb[:, 512:1024], start=True, stop=True), mmC2)
                wait("vector", mmC2, mn)
                op("vector", lambda e, cl_=cloc: e.tensor_copy(
                    cs_tile[:, cl_ * 1024:cl_ * 1024 + 512], pE[:, 0:512]), vC2)
                cn = op("vector", lambda e, cl_=cloc: e.tensor_copy(
                    cs_tile[:, cl_ * 1024 + 512:(cl_ + 1) * 1024], pE2[:, 0:512]), vC2)
                gset("pE", (vC2, cn))
                gset("pE2", (vC2, cn))
            wait("tensor", v3, v3.n)
            for m in range(32 if p3sub >= 3 else 0):
                sl1, sl2 = (2 * m) % 4, (2 * m + 1) % 4
                if tj == 0 and m == 0:
                    wait("sync", cl, 16)
                    wait("sync", clB, 16)
                gwait("sync", ("l1", sl1))
                wna = dop("sync", lambda e, mm_=m, s=sl1: e.dma_start(
                    l1slot(s), w1b_d[mm_]), wl)
                gwait("sync", ("l1", sl2))
                wn = dop("sync", lambda e, mm_=m, s=sl2: e.dma_start(
                    l1slot(s), w2b_d[mm_]), wlB)
                pa = pA if m % 2 == 0 else pA2
                pb = pB if m % 2 == 0 else pB2
                ka = "pAL" if m % 2 == 0 else "pA2L"
                kb = "pBL" if m % 2 == 0 else "pB2L"
                wait("tensor", wl, wna)
                wait("tensor", wlB, wn)
                if m == 0:
                    gwait("tensor", "pA01")
                    gwait("tensor", "pB0")
                gwait("tensor", ka)
                gwait("tensor", kb)
                for k in range(8):
                    op("tensor", lambda e, p=pa, s=sl1, kk=k: e.matmul(
                        p[:, 0:512], l1slot(s)[:, kk * 128:(kk + 1) * 128],
                        xnbt[:, kk * 512:(kk + 1) * 512],
                        start=(kk == 0), stop=(kk == 7)), None)
                for k in range(8):
                    op("tensor", lambda e, p=pb, s=sl2, kk=k: e.matmul(
                        p[:, 0:512], l1slot(s)[:, kk * 128:(kk + 1) * 128],
                        xnbt[:, kk * 512:(kk + 1) * 512],
                        start=(kk == 0), stop=(kk == 7)), mmL if k == 7 else None)
                gset(("l1", sl1), (mmL, mmL.n))
                gset(("l1", sl2), (mmL, mmL.n))
                wait("scalar", mmL, mmL.n)
                gwait("scalar", ("th", m % 2))
                gwait("scalar", "usilu_u")
                thn = op("scalar", lambda e, p=pa, g=m, sl=m % 2: e.activation(
                    thbuf[:, sl * 512:(sl + 1) * 512], p[:, 0:512], AF.Tanh,
                    scale=s1h[:, g:g + 1]), scL)
                un = op("scalar", lambda e, p=pa, g=m: e.activation(
                    ubuf[:], p[:, 0:512], AF.Copy, scale=s1h2[:, g:g + 1]), scL)
                gset(ka, (scL, un))
                wait("vector", scL, un)
                gwait("vector", "usilu")
                if m == 0:
                    gwait("vector", "hbuf_w")
                sn = op("vector", lambda e, sl=m % 2: e.scalar_tensor_tensor(
                    silubuf[:], thbuf[:, sl * 512:(sl + 1) * 512], 1.0, ubuf[:],
                    op0=OP.add, op1=OP.mult), vL)
                gset(("th", m % 2), (vL, sn))
                gset("usilu_u", (vL, sn))
                wait("vector", vL, sn)
                hn = op("vector", lambda e, p=pb, g=m: e.tensor_tensor(
                    hbuf[:, g * 512:(g + 1) * 512], silubuf[:], p[:, 0:512],
                    op=OP.mult), vL)
                gset(kb, (vL, hn))
                gset("usilu", (vL, hn))
            L1_END_VL = vL.n
            for lay in range(2 if p3sub >= 5 else (1 if p3sub >= 4 else 0)):
                wsrc = h1b_d if lay == 0 else h2b_d
                sc_ap = h1spk if lay == 0 else h2spk
                rsrc = hbuf if lay == 0 else g1buf
                wdst = g1buf if lay == 0 else hbuf
                H1_END_SCH = scH.n
                for m in range(32):
                    hs = hring_i[0] % 2
                    hring_i[0] += 1
                    if tj == 0 and m == 0:
                        wait("sync", clC if lay == 0 else clD, 16)
                    gwait("sync", ("hr", hs))
                    wn = dop("sync", lambda e, mm_=m, s=hs, w=wsrc: e.dma_start(
                        hslot(s), w[mm_]), wh)
                    pe_ = pE if m % 2 == 0 else pE2
                    ke = "pE" if m % 2 == 0 else "pE2"
                    wait("tensor", wh, wn)
                    if lay == 0 and m == 0:
                        wait("tensor", vL, L1_END_VL)
                    if lay == 1 and m == 0:
                        wait("tensor", scH, H1_END_SCH)
                    gwait("tensor", ke)
                    for k in range(32):
                        op("tensor", lambda e, p=pe_, s=hs, kk=k, r=rsrc: e.matmul(
                            p[:, 0:512], hslot(s)[:, kk * 128:(kk + 1) * 128],
                            r[:, kk * 512:(kk + 1) * 512],
                            start=(kk == 0), stop=(kk == 31)),
                            mmH if k == 31 else None)
                    gset(("hr", hs), (mmH, mmH.n))
                    wait("scalar", mmH, mmH.n)
                    gn = op("scalar", lambda e, p=pe_, g=m, w=wdst, s=sc_ap: e.activation(
                        w[:, g * 512:(g + 1) * 512], p[:, 0:512], AF.Gelu,
                        scale=s[:, g:g + 1]), scH)
                    gset(ke, (scH, gn))
            H2_END_SCH = scH.n
            for mo in range(8 if p3sub >= 6 else 0):
                hs = hring_i[0] % 2
                hring_i[0] += 1
                if tj == 0 and mo == 0:
                    wait("sync", cv, 16)
                gwait("sync", ("hr", hs))
                wn = dop("sync", lambda e, mm_=mo, s=hs: e.dma_start(
                    hslot(s), w3b_d[mm_]), wh)
                pw = pG if mo % 2 == 0 else pH
                kw = "pG" if mo % 2 == 0 else "pH"
                wait("tensor", wh, wn)
                if mo == 0:
                    wait("tensor", scH, H2_END_SCH)
                gwait("tensor", kw)
                for k in range(32):
                    op("tensor", lambda e, p=pw, s=hs, kk=k: e.matmul(
                        p[:, 0:512], hslot(s)[:, kk * 128:(kk + 1) * 128],
                        hbuf[:, kk * 512:(kk + 1) * 512],
                        start=(kk == 0), stop=(kk == 31)), mmW if k == 31 else None)
                gset(("hr", hs), (mmW, mmW.n))
                wait("vector", mmW, mmW.n)
                if mo == 0:
                    gwait("vector", "ffn_fm")
                en = op("vector", lambda e, p=pw, g=mo: e.tensor_scalar(
                    ffn_fm[:, g * 512:(g + 1) * 512], p[:, 0:512],
                    w3spk[:, g:g + 1], None, op0=OP.mult), vW)
                gset(kw, (vW, en))
            W3_END_VW = vW.n
            gset("hbuf_w", (mmW, mmW.n))
            for cloc in range(4):
                c = 4 * tj + cloc
                r0 = (c % 16) * 128
                slot = cloc % 2
                pF, pF2 = (pE, pE2) if cloc % 2 == 0 else (pG, pH)
                kF, kF2 = ("pE", "pE2") if cloc % 2 == 0 else ("pG", "pH")
                gwait("gpsimd", ("xrb4", slot))
                ldn = dop("gpsimd", lambda e, bb_=tb, rr=r0, sl=slot: e.dma_start(
                    xrb4[:, sl * DM:(sl + 1) * DM], x_d[bb_, rr:rr + 128, :]), x4)
                wait("tensor", vW, W3_END_VW)
                gwait("tensor", kF)
                gwait("tensor", kF2)
                for mo in range(8):
                    bank = pF if mo < 4 else pF2
                    cc0 = (mo % 4) * 128
                    op("tensor", lambda e, bk=bank, c0=cc0, g=mo, cl_=cloc: e.transpose(
                        bk[:, c0:c0 + 128],
                        ffn_fm[:, g * 512 + cl_ * 128:g * 512 + (cl_ + 1) * 128],
                        eye_s[:]), mmF if mo == 7 else None)
                wait("vector", mmF, mmF.n)
                wait("vector", vC2, vC2.n)
                wait("vector", x4, ldn)
                gwait("vector", "t1t2")
                t1n = op("vector", lambda e, sl=slot, cc=c: e.tensor_scalar(
                    t1b[:], xrb4[:, sl * DM:(sl + 1) * DM],
                    muvar[:, 2 * cc:2 * cc + 1], rstd[:, cc:cc + 1],
                    op0=OP.subtract, op1=OP.mult), v7)
                wait("gpsimd", v7, t1n)
                t2n = op("gpsimd", lambda e: e.tensor_tensor(
                    t2b[:], t1b[:], d1b[:], op=OP.mult), p7)
                wait("vector", p7, t2n)
                yn = op("vector", lambda e, cl_=cloc: e.tensor_tensor(
                    t1b[:], t2b[:], cs_tile[:, cl_ * 1024:(cl_ + 1) * 1024],
                    op=OP.add), v7)
                wait("vector", v7, yn)
                ln_ = op("vector", lambda e, cc=c: e.tensor_scalar(
                    t2b[:], t1b[:], muyvar[:, 2 * cc:2 * cc + 1],
                    rstdy[:, cc:cc + 1], op0=OP.subtract, op1=OP.mult), v7)
                wait("gpsimd", v7, ln_)
                gn = op("gpsimd", lambda e: e.tensor_tensor(
                    t1b[:], t2b[:], gbc[:], op=OP.mult), p7)
                wait("gpsimd", p7, gn)
                an = op("gpsimd", lambda e: e.tensor_tensor(
                    t2b[:], t1b[:], bbc[:], op=OP.add), p7)
                wait("gpsimd", p7, an)
                s1n = op("gpsimd", lambda e, sl=slot: e.tensor_tensor(
                    t1b[:], t2b[:], xrb4[:, sl * DM:(sl + 1) * DM], op=OP.add), p7)
                gset(("xrb4", slot), (p7, s1n))
                wait("vector", p7, s1n)
                gwait("vector", ("out", slot))
                o1n = op("vector", lambda e, sl=slot, p=pF: e.tensor_tensor(
                    out_sb[:, sl * DM:sl * DM + 512], t1b[:, 0:512],
                    p[:, 0:512], op=OP.add), v7)
                gset(kF, (v7, o1n))
                o2n = op("vector", lambda e, sl=slot, p=pF2: e.tensor_tensor(
                    out_sb[:, sl * DM + 512:(sl + 1) * DM], t1b[:, 512:1024],
                    p[:, 0:512], op=OP.add), vout)
                gset(kF2, (vout, o2n))
                gset("t1t2", (v7, o1n), (vout, o2n))
                wait("gpsimd", v7, o1n)
                wait("gpsimd", vout, o2n)
                sn = dop("gpsimd", lambda e, bb_=tb, rr=r0, sl=slot: e.dma_start(
                    out_d[bb_, rr:rr + 128, :], out_sb[:, sl * DM:(sl + 1) * DM]),
                    sout)
                gset(("out", slot), (sout, sn))
            gset("cs_tile_w", (v7, v7.n))
            gset("xnbt", (mmL, mmL.n))
            gset("ffn_fm", (mmF, mmF.n))

        wait("gpsimd", sout, sout.n)
        if debug:
            wait("gpsimd", sdbg, sdbg.n)

        with nc.Block("ph2") as blk2:
            materialize(blk2, ST2)

    return nc


def _tile_io(w):
    wt = np.asarray(w, dtype=np.float32).T
    v = wt.reshape(8, 128, 32, 128)
    return np.ascontiguousarray(v.transpose(2, 1, 0, 3).reshape(32, 128, 8 * 128))


def _tile_hh(w):
    wt = np.asarray(w, dtype=np.float32).T
    I, O = wt.shape
    nm = O // 128
    v = wt.reshape(I // 1024, 8, 128, nm, 128)
    return np.ascontiguousarray(v.transpose(3, 2, 0, 1, 4).reshape(nm, 128, I))


def _prep_inputs(inputs):
    f32c = lambda a: np.ascontiguousarray(np.asarray(a, dtype=np.float32))
    x = f32c(inputs["L0_attn_x"])
    common = {
        "at": f32c(np.asarray(inputs["L0_attn_A"]).T),
        "bt": f32c(np.asarray(inputs["L0_attn_B"]).T),
        "ct": f32c(np.asarray(inputs["L0_attn_C"]).T),
        "drow": f32c(np.asarray(inputs["L0_attn_D"]).reshape(1, DM)),
        "grow": f32c(np.asarray(inputs["L0_attn_gamma"]).reshape(1, DM)),
        "brow": f32c(np.asarray(inputs["L0_attn_beta"]).reshape(1, DM)),
        "w1t": _tile_io(inputs["L0_ffn_w1_w"]),
        "w2t": _tile_io(inputs["L0_ffn_w2_w"]),
        "h1t": _tile_hh(inputs["L0_ffn_h1_w"]),
        "h2t": _tile_hh(inputs["L0_ffn_h2_w"]),
        "w3t": _tile_hh(inputs["L0_ffn_w3_w"]),
        "s1pk": f32c(np.asarray(inputs["L0_ffn_w1_s"]).reshape(32, 128).T),
        "s2pk": f32c(np.asarray(inputs["L0_ffn_w2_s"]).reshape(32, 128).T),
        "h1spk": f32c(np.asarray(inputs["L0_ffn_h1_s"]).reshape(32, 128).T),
        "h2spk": f32c(np.asarray(inputs["L0_ffn_h2_s"]).reshape(32, 128).T),
        "w3spk": f32c(np.asarray(inputs["L0_ffn_w3_s"]).reshape(8, 128).T),
        "eye": np.eye(128, dtype=np.float32),
        "ones": np.ones((1, 128), dtype=np.float32),
    }
    in_maps = []
    for c in range(8):
        m = dict(common)
        m["x"] = np.ascontiguousarray(x[2 * c:2 * c + 2])
        in_maps.append(m)
    return in_maps


def kernel(**inputs):
    in_maps = _prep_inputs(inputs)
    nc = _build(debug=False)
    res = run_bass_kernel_spmd(nc, in_maps, core_ids=list(range(8)))
    return np.concatenate([r["out"] for r in res.results], axis=0)


# version 12
# speedup vs baseline: 1.0055x; 1.0055x over previous
import contextlib
import numpy as np
import ml_dtypes
import concourse.bass as bass
import concourse.mybir as mybir
from concourse.bass_utils import run_bass_kernel_spmd

F32 = mybir.dt.float32
BF16 = mybir.dt.bfloat16
AF = mybir.ActivationFunctionType
OP = mybir.AluOpType

SEQ = 2048
DM = 1024
DC = 8
HID = 4096
HC = 32
NTILE = 8
NCH = 32
EPS = 1e-5


class Sem:
    def __init__(self, h):
        self.h = h
        self.n = 0


def _build(debug=False, ntile=NTILE, ncores=8, p3sub=7):
    nc = bass.Bass("TRN2", target_bir_lowering=False)

    x_d = nc.dram_tensor("x", [2, SEQ, DM], F32, kind="ExternalInput").ap()
    at_d = nc.dram_tensor("at", [128, 128], F32, kind="ExternalInput").ap()
    bt_d = nc.dram_tensor("bt", [DM, 128], F32, kind="ExternalInput").ap()
    ct_d = nc.dram_tensor("ct", [128, DM], F32, kind="ExternalInput").ap()
    drow_d = nc.dram_tensor("drow", [1, DM], F32, kind="ExternalInput").ap()
    grow_d = nc.dram_tensor("grow", [1, DM], F32, kind="ExternalInput").ap()
    brow_d = nc.dram_tensor("brow", [1, DM], F32, kind="ExternalInput").ap()
    w1t_d = nc.dram_tensor("w1t", [32, 128, 1024], F32, kind="ExternalInput").ap()
    w2t_d = nc.dram_tensor("w2t", [32, 128, 1024], F32, kind="ExternalInput").ap()
    h1t_d = nc.dram_tensor("h1t", [32, 128, 4096], F32, kind="ExternalInput").ap()
    h2t_d = nc.dram_tensor("h2t", [32, 128, 4096], F32, kind="ExternalInput").ap()
    w3t_d = nc.dram_tensor("w3t", [8, 128, 4096], F32, kind="ExternalInput").ap()
    s1_d = nc.dram_tensor("s1pk", [128, 32], F32, kind="ExternalInput").ap()
    s2_d = nc.dram_tensor("s2pk", [128, 32], F32, kind="ExternalInput").ap()
    h1s_d = nc.dram_tensor("h1spk", [128, 32], F32, kind="ExternalInput").ap()
    h2s_d = nc.dram_tensor("h2spk", [128, 32], F32, kind="ExternalInput").ap()
    w3s_d = nc.dram_tensor("w3spk", [128, 8], F32, kind="ExternalInput").ap()
    eye_d = nc.dram_tensor("eye", [128, 128], F32, kind="ExternalInput").ap()
    ones_d = nc.dram_tensor("ones", [1, 128], F32, kind="ExternalInput").ap()
    out_d = nc.dram_tensor("out", [2, SEQ, DM], F32, kind="ExternalOutput").ap()

    w1b_d = nc.dram_tensor("w1b", [32, 128, 1024], BF16, kind="Internal").ap()
    w2b_d = nc.dram_tensor("w2b", [32, 128, 1024], BF16, kind="Internal").ap()
    h1b_d = nc.dram_tensor("h1b", [32, 128, 4096], BF16, kind="Internal").ap()
    h2b_d = nc.dram_tensor("h2b", [32, 128, 4096], BF16, kind="Internal").ap()
    w3b_d = nc.dram_tensor("w3b", [8, 128, 4096], BF16, kind="Internal").ap()

    if debug:
        dbg_xb = nc.dram_tensor("dbg_xb", [128, 4352], F32, kind="ExternalOutput").ap()
        dbg_st = nc.dram_tensor("dbg_st", [128, 4352], BF16, kind="ExternalOutput").ap()
        dbg_mv = nc.dram_tensor("dbg_mv", [128, 64], F32, kind="ExternalOutput").ap()
        dbg_rs = nc.dram_tensor("dbg_rs", [128, 32], F32, kind="ExternalOutput").ap()
        dbg_myv = nc.dram_tensor("dbg_myv", [128, 64], F32, kind="ExternalOutput").ap()
        dbg_rsy = nc.dram_tensor("dbg_rsy", [128, 32], F32, kind="ExternalOutput").ap()

    ENGS = ("sync", "tensor", "vector", "scalar", "gpsimd")
    ST1 = {e: [] for e in ENGS}
    ST2 = {e: [] for e in ENGS}
    CUR = [ST1]
    WAITED = {e: {} for e in ENGS}
    GUARD = {}

    def em(eng, fn):
        CUR[0][eng].append(fn)

    def wait(eng, sem, val):
        if val <= 0:
            return
        if WAITED[eng].get(sem, 0) >= val:
            return
        WAITED[eng][sem] = val
        em(eng, lambda e, h=sem.h, v=val: e.wait_ge(h, v))

    def op(eng, fn, sem=None, k=1):
        if sem is not None:
            sem.n += k
            em(eng, lambda e, f=fn, h=sem.h, kk=k: f(e).then_inc(h, kk))
            return sem.n
        em(eng, fn)
        return None

    def dop(eng, fn, sem):
        wait(eng, sem, sem.n)
        return op(eng, fn, sem, 16)

    def gwait(eng, key):
        for s, v in GUARD.get(key, ()):
            wait(eng, s, v)

    def gset(key, *pairs):
        GUARD[key] = list(pairs)

    def materialize(blk, streams):
        @blk.sync
        def _(e):
            for f in streams["sync"]:
                f(e)

        @blk.tensor
        def _(e):
            for f in streams["tensor"]:
                f(e)

        @blk.vector
        def _(e):
            for f in streams["vector"]:
                f(e)

        @blk.scalar
        def _(e):
            for f in streams["scalar"]:
                f(e)

        @blk.gpsimd
        def _(e):
            for f in streams["gpsimd"]:
                f(e)

    es = contextlib.ExitStack()
    with es:
        names = ("cld", "cv0", "cmm", "x0", "v0", "sc0", "v0b", "x1",
                 "v1", "mmT", "v2", "mmB", "xb", "ac", "smm",
                 "cl", "clB", "clC", "clD", "cv", "cs", "csB", "csC", "csD", "x2", "mmC", "v5g", "v5",
                 "sc2", "v5b", "wl", "wlB", "wh", "mmL", "scL", "vL",
                 "mmH", "scH", "mmW", "vW", "mmC2", "vC2", "mmT3",
                 "v3", "x3", "x4", "mmF", "v7", "vout", "sout", "sdbg", "vsc", "p7")
        SM = {n: Sem(es.enter_context(nc.semaphore(n))) for n in names}
        (cld, cv0, cmm, x0, v0, sc0, v0b, x1, v1, mmT, v2, mmB, xb, ac,
         smm, cl, clB, clC, clD, cv, cs, csB, csC, csD, x2, mmC, v5g, v5,
         sc2, v5b, wl, wlB, wh, mmL, scL, vL, mmH, scH, mmW, vW, mmC2,
         vC2, mmT3, v3, x3, x4, mmF, v7, vout, sout, sdbg, vsc, p7) = (
            SM[n] for n in names)

        sbT = lambda n, shp, dt: es.enter_context(nc.sbuf_tensor(n, shp, dt))
        states = sbT("states", [128, 4352], BF16)
        xcol = lambda b, t: b * 2176 + 64 + t
        xn_tm = sbT("xn_tm", [128, 2 * DM], F32)
        xrb = sbT("xrb", [128, 2 * DM], F32)
        t1b = sbT("t1b", [128, DM], F32)
        t2b = sbT("t2b", [128, DM], F32)
        d1b = sbT("d1b", [128, DM], F32)
        gbc = sbT("gbc", [128, DM], F32)
        bbc = sbT("bbc", [128, DM], F32)
        ctb = sbT("ctb", [128, DM], BF16)
        eye_s = sbT("eye_s", [128, 128], F32)
        rowbuf = sbT("rowbuf", [1, DM], F32)
        ones_s = sbT("ones_s", [1, 128], F32)
        eps_s = sbT("eps_s", [128, 1], F32)
        s1pk = sbT("s1pk_s", [128, 32], F32)
        s2pk = sbT("s2pk_s", [128, 32], F32)
        h1spk = sbT("h1spk_s", [128, 32], F32)
        h2spk = sbT("h2spk_s", [128, 32], F32)
        w3spk = sbT("w3spk_s", [128, 8], F32)
        s1h = sbT("s1h", [128, 32], F32)
        s1h2 = sbT("s1h2", [128, 32], F32)
        muvar = sbT("muvar", [128, 64], F32)
        rstd = sbT("rstd", [128, 32], F32)
        sqtmp = sbT("sqtmp", [128, 32], F32)
        muyvar = sbT("muyvar", [128, 64], F32)
        rstdy = sbT("rstdy", [128, 32], F32)
        bnb = sbT("bnb", [128, 12], F32)

        psT = lambda n: es.enter_context(nc.psum_tensor(n, [128, 512], F32))
        pA, pA2, pB, pB2, pE, pE2, pG, pH = (psT(n) for n in
            ("pA", "pA2", "pB", "pB2", "pE", "pE2", "pG", "pH"))

        ph1 = contextlib.ExitStack()
        sb1 = lambda n, shp, dt: ph1.enter_context(nc.sbuf_tensor(n, shp, dt))
        xB = sb1("xB", [128, 4352], F32)
        sctmp = sb1("sctmp", [128, 16], F32)
        atb = sb1("atb", [128, 128], BF16)
        btb = sb1("btb", [128, DM], BF16)
        xnbw = sb1("xnbw", [128, DC * 128], BF16)
        cin = sb1("cin", [128, 2304], F32)
        cstmp = sb1("cstmp", [128, DM], BF16)

        dop("sync", lambda e: e.dma_start(cin[:, 0:128], at_d), cld)
        bt3 = bt_d.rearrange("(k p) n -> p k n", p=128)
        cin_bt = cin[:, 128:1152].rearrange("p (k n) -> p k n", n=128)
        dop("sync", lambda e: e.dma_start(cin_bt, bt3), cld)
        dop("sync", lambda e: e.dma_start(cin[:, 1152:2176], ct_d), cld)
        dop("sync", lambda e: e.dma_start(eye_s[:], eye_d), cld)
        dop("sync", lambda e: e.dma_start(ones_s[:], ones_d), cld)
        for dd, ss in ((s1_d, s1pk), (s2_d, s2pk), (h1s_d, h1spk),
                       (h2s_d, h2spk), (w3s_d, w3spk)):
            dop("sync", lambda e, d=dd, s=ss: e.dma_start(s[:], d), cld)
        wait("vector", cld, 160)
        op("vector", lambda e: e.memset(eps_s[:], EPS), cv0)
        op("vector", lambda e: e.memset(xB[:], 0.0), cv0)
        op("vector", lambda e: e.memset(states[:], 0.0), cv0)
        op("vector", lambda e: e.tensor_copy(atb[:], cin[:, 0:128]), cv0)
        op("vector", lambda e: e.tensor_copy(btb[:], cin[:, 128:1152]), cv0)
        op("vector", lambda e: e.tensor_copy(ctb[:], cin[:, 1152:2176]), cv0)
        op("vector", lambda e: e.tensor_scalar(
            s1h[:], s1pk[:], 0.5, None, op0=OP.mult), cv0)
        wait("vector", cv0, cv0.n)
        op("vector", lambda e: e.tensor_tensor(
            s1h2[:], s1h[:], s2pk[:], op=OP.mult), cv0)
        CV0_CONSTS = cv0.n

        for rd, dst, plus1 in ((drow_d, d1b, True), (grow_d, gbc, False),
                               (brow_d, bbc, False)):
            gwait("sync", "rowbuf")
            ldn = dop("sync", lambda e, d=rd: e.dma_start(rowbuf[:], d), cld)
            for hh in range(2):
                wait("tensor", cld, ldn)
                gwait("tensor", "pG")
                mn = op("tensor", lambda e, h=hh: e.matmul(
                    pG[:, 0:512], ones_s[0:1, :],
                    rowbuf[0:1, h * 512:(h + 1) * 512],
                    start=True, stop=True), cmm)
                wait("vector", cmm, mn)
                if plus1:
                    en = op("vector", lambda e, h=hh, ds=dst: e.tensor_scalar(
                        ds[:, h * 512:(h + 1) * 512], pG[:, 0:512],
                        1.0, None, op0=OP.add), cv0)
                else:
                    en = op("vector", lambda e, h=hh, ds=dst: e.tensor_copy(
                        ds[:, h * 512:(h + 1) * 512], pG[:, 0:512]), cv0)
                gset("pG", (cv0, en))
            gset("rowbuf", (cmm, cmm.n))
        CV0_ALL = cv0.n

        for i in range(32):
            j, b = i // 2, i % 2
            c = b * 16 + j
            slot = i % 2
            gwait("gpsimd", ("xrb", slot))
            ldn = dop("gpsimd", lambda e, bb_=b, r0=j * 128, sl=slot: e.dma_start(
                xrb[:, sl * DM:(sl + 1) * DM], x_d[bb_, r0:r0 + 128, :]), x0)
            wait("vector", x0, ldn)
            gwait("vector", "bnb")
            op("vector", lambda e, sl=slot: e.bn_stats(
                bnb[:, 0:6], xrb[:, sl * DM:sl * DM + 512]), v0)
            sn2 = op("vector", lambda e, sl=slot: e.bn_stats(
                bnb[:, 6:12], xrb[:, sl * DM + 512:(sl + 1) * DM]), v0)
            wait("vector", v0, sn2)
            an = op("vector", lambda e, cc=c: e.bn_aggr(
                muvar[:, 2 * cc:2 * cc + 2], bnb[:]), v0)
            gset(("xrb", slot), (v0, sn2))
            gset("bnb", (v0, an))

        wait("scalar", v0, v0.n)
        wait("scalar", cv0, 1)
        qn = op("scalar", lambda e: e.activation(
            sqtmp[:], muvar[:, 1:64:2], AF.Sqrt, bias=eps_s[:]), sc0)
        wait("vector", sc0, qn)
        op("vector", lambda e: e.reciprocal(rstd[:], sqtmp[:]), v0b)
        gset("sqtmp", (v0b, v0b.n))

        wait("tensor", cld, 160)
        wait("tensor", cv0, CV0_CONSTS)
        wait("vector", v0, 96)
        for i in range(32):
            j, b = i // 2, i % 2
            c = b * 16 + j
            slot = i % 2
            gwait("gpsimd", ("xrb", slot))
            ldn = dop("gpsimd", lambda e, bb_=b, r0=j * 128, sl=slot: e.dma_start(
                xrb[:, sl * DM:(sl + 1) * DM], x_d[bb_, r0:r0 + 128, :]), x1)
            wait("vector", x1, ldn)
            wait("vector", v0b, 1)
            gwait("vector", ("xn_tm", slot))
            tn = op("vector", lambda e, sl=slot, cc=c: e.tensor_scalar(
                xn_tm[:, sl * DM:(sl + 1) * DM], xrb[:, sl * DM:(sl + 1) * DM],
                muvar[:, 2 * cc:2 * cc + 1], rstd[:, cc:cc + 1],
                op0=OP.subtract, op1=OP.mult), v1)
            gset(("xrb", slot), (v1, tn))
            wait("tensor", v1, tn)
            gwait("tensor", "pA01")
            for k in range(8):
                bank = pA if k < 4 else pA2
                cc0 = (k % 4) * 128
                op("tensor", lambda e, bk=bank, c0=cc0, sl=slot, kk=k: e.transpose(
                    bk[:, c0:c0 + 128],
                    xn_tm[:, sl * DM + kk * 128:sl * DM + (kk + 1) * 128],
                    eye_s[:]), mmT if k == 7 else None)
            gset(("xn_tm", slot), (mmT, mmT.n))
            wait("vector", mmT, mmT.n)
            gwait("vector", "xnbw")
            for k in range(8):
                bank = pA if k < 4 else pA2
                cc0 = (k % 4) * 128
                op("vector", lambda e, bk=bank, c0=cc0, kk=k: e.tensor_copy(
                    xnbw[:, kk * 128:(kk + 1) * 128], bk[:, c0:c0 + 128]),
                    v2 if k == 7 else None)
            gset("pA01", (v2, v2.n))
            wait("tensor", v2, v2.n)
            gwait("tensor", "pB0")
            for k in range(8):
                op("tensor", lambda e, kk=k: e.matmul(
                    pB[:, 0:128], btb[:, kk * 128:(kk + 1) * 128],
                    xnbw[:, kk * 128:(kk + 1) * 128],
                    start=(kk == 0), stop=(kk == 7)), mmB if k == 7 else None)
            gset("xnbw", (mmB, mmB.n))
            wait("vector", mmB, mmB.n)
            op("vector", lambda e, b_=b, jj=j: e.tensor_copy(
                xB[:, xcol(b_, jj * 128):xcol(b_, jj * 128) + 128], pB[:, 0:128]), xb)
            gset("pB0", (xb, xb.n))

        CSEM = {"w1": cl, "w2": clB, "h1": clC, "h2": clD, "w3": cv}
        for nm_, s_, d_ in (("w1", w1t_d, w1b_d), ("w2", w2t_d, w2b_d),
                            ("h1", h1t_d, h1b_d), ("h2", h2t_d, h2b_d),
                            ("w3", w3t_d, w3b_d)):
            op("gpsimd", lambda e, ss=s_, dd=d_: e.dma_start(dd, ss), CSEM[nm_], 16)

        def emit_x2(j):
            for b in (0, 1):
                slot = b
                gwait("scalar", ("xrb", slot))
                ldn = dop("scalar", lambda e, bb_=b, r0=j * 128, sl=slot: e.dma_start(
                    xrb[:, sl * DM:(sl + 1) * DM], x_d[bb_, r0:r0 + 128, :]), x2)
                X2N[(j, b)] = ldn

        def emit_p2(j):
            for b in (0, 1):
                c = b * 16 + j
                slot = b
                wait("tensor", ac, NBURN + (j % 4) * 128 + 128)
                gwait("tensor", "pG")
                gwait("tensor", "pH")
                sc0_ = xcol(b, (c % 16) * 128)
                op("tensor", lambda e, s0=sc0_: e.matmul(
                    pG[:, 0:512], states[:, s0:s0 + 128],
                    ctb[:, 0:512], start=True, stop=True), mmC)
                mn = op("tensor", lambda e, s0=sc0_: e.matmul(
                    pH[:, 0:512], states[:, s0:s0 + 128],
                    ctb[:, 512:1024], start=True, stop=True), mmC)
                wait("vector", mmC, mn)
                gwait("vector", "cstmp")
                op("vector", lambda e: e.tensor_copy(cstmp[:, 0:512], pG[:, 0:512]), v5g)
                gn = op("vector", lambda e: e.tensor_copy(
                    cstmp[:, 512:1024], pH[:, 0:512]), v5g)
                gset("pG", (v5g, gn))
                gset("pH", (v5g, gn))
                wait("vector", x2, X2N[(j, b)])
                wait("vector", cv0, CV0_ALL)
                gwait("vector", "t1t2")
                t1n = op("vector", lambda e, sl=slot, cc=c: e.tensor_scalar(
                    t1b[:], xrb[:, sl * DM:(sl + 1) * DM],
                    muvar[:, 2 * cc:2 * cc + 1], rstd[:, cc:cc + 1],
                    op0=OP.subtract, op1=OP.mult), v5)
                gset(("xrb", slot), (v5, t1n))
                wait("vector", v5, t1n)
                t2n = op("vector", lambda e: e.tensor_tensor(
                    t2b[:], t1b[:], d1b[:], op=OP.mult), v5)
                wait("vector", v5, t2n)
                wait("vector", v5g, gn)
                yn = op("vector", lambda e: e.tensor_tensor(
                    t1b[:], t2b[:], cstmp[:], op=OP.add), v5)
                gset("cstmp", (v5, yn))
                wait("vector", v5, yn)
                gwait("vector", "bnb")
                op("vector", lambda e: e.bn_stats(bnb[:, 0:6], t1b[:, 0:512]), v5)
                s2n = op("vector", lambda e: e.bn_stats(
                    bnb[:, 6:12], t1b[:, 512:1024]), v5)
                wait("vector", v5, s2n)
                an = op("vector", lambda e, cc=c: e.bn_aggr(
                    muyvar[:, 2 * cc:2 * cc + 2], bnb[:]), v5)
                gset("bnb", (v5, an))
                gset("t1t2", (v5, an))

        X2N = {}
        NBURN = 48
        st_h = states.tensor if hasattr(states, "tensor") else states
        xb_h = xB.tensor if hasattr(xB, "tensor") else xB
        def stv(off):
            return bass.AP(st_h, off, [[4352, 128], [2176, 2], [512, 4]])
        def xbv(off):
            return bass.AP(xb_h, off, [[4352, 128], [2176, 2], [512, 4]])
        wait("tensor", cv0, CV0_CONSTS)
        wait("vector", xb, 32)
        wait("scalar", xb, 32)
        for n in range(NBURN + 512):
            i = n - NBURN
            bank = pE if n % 2 == 0 else pE2
            s8 = (n % 2) * 8
            wait("tensor", ac, n)
            mn = op("tensor", lambda e, bk=bank, off=63 + i: e.matmul(
                bk[:, 0:8], atb[:], stv(off), start=True, stop=True), smm)
            wait("vector", smm, mn)
            an = op("vector", lambda e, bk=bank, ss=s8, off=64 + i: e.tensor_tensor(
                sctmp[:, ss:ss + 8], bk[:, 0:8], xbv(off), op=OP.add), vsc)
            wait("scalar", vsc, an)
            op("scalar", lambda e, ss=s8, off=64 + i: e.activation(
                stv(off), sctmp[:, ss:ss + 8], AF.Tanh), ac)
        gset("pE", (vsc, vsc.n))
        gset("pE2", (vsc, vsc.n))
        for j in range(16):
            emit_x2(j)
            emit_p2(j)
        gset("pE", (ac, ac.n))
        gset("pE2", (ac, ac.n))

        wait("scalar", v5, v5.n)
        gwait("scalar", "sqtmp")
        qn = op("scalar", lambda e: e.activation(
            sqtmp[:], muyvar[:, 1:64:2], AF.Sqrt, bias=eps_s[:]), sc2)
        wait("vector", sc2, qn)
        op("vector", lambda e: e.reciprocal(rstdy[:], sqtmp[:]), v5b)

        if debug:
            wait("gpsimd", xb, 32)
            dop("gpsimd", lambda e: e.dma_start(dbg_xb, xB[:]), sdbg)
            wait("gpsimd", ac, ac.n)
            dop("gpsimd", lambda e: e.dma_start(dbg_st, states[:]), sdbg)
            wait("gpsimd", v0b, 1)
            dop("gpsimd", lambda e: e.dma_start(dbg_mv, muvar[:]), sdbg)
            dop("gpsimd", lambda e: e.dma_start(dbg_rs, rstd[:]), sdbg)
            wait("gpsimd", v5b, 1)
            dop("gpsimd", lambda e: e.dma_start(dbg_myv, muyvar[:]), sdbg)
            dop("gpsimd", lambda e: e.dma_start(dbg_rsy, rstdy[:]), sdbg)

        with nc.Block("ph1") as blk1:
            materialize(blk1, ST1)
        ph1.close()

        CUR[0] = ST2
        sb2 = lambda n, shp, dt: es.enter_context(nc.sbuf_tensor(n, shp, dt))
        xnbt = sb2("xnbt", [128, DC * 512], BF16)
        hbuf = sb2("hbuf", [128, HC * 512], BF16)
        g1buf = sb2("g1buf", [128, HC * 512], BF16)
        hring = sb2("hring", [128, 2 * 4096], BF16)
        l1ring = sb2("l1ring", [128, 4 * 1024], BF16)
        ffn_fm = sb2("ffn_fm", [128, 8 * 512], F32)
        cs_tile = sb2("cs_tile", [128, 4 * 1024], BF16)
        xrb4 = sb2("xrb4", [128, 2 * DM], F32)
        out_sb = sb2("out_sb", [128, 2 * DM], F32)
        thbuf = sb2("thbuf", [128, 2 * 512], F32)
        ubuf = sb2("ubuf", [128, 512], F32)
        silubuf = sb2("silubuf", [128, 512], F32)

        hslot = lambda s: hring[:, s * 4096:(s + 1) * 4096]
        l1slot = lambda s: l1ring[:, s * 1024:(s + 1) * 1024]

        PH1_FINAL = [(s_, s_.n) for s_ in (
            cld, cv0, cmm, x0, v0, sc0, v0b, x1, v1, mmT, v2, mmB, xb,
            ac, smm, cl, clB, clC, clD, cv, cs, csB, csC, csD, x2, mmC,
            v5g, v5, sc2, v5b, vsc)]
        for eng in ("sync", "vector", "scalar", "gpsimd", "tensor"):
            for s_, v_ in PH1_FINAL:
                wait(eng, s_, v_)

        hring_i = [0]

        def emit_g(tj, cloc):
            tb = tj // 4
            c = 4 * tj + cloc
            r0 = (c % 16) * 128
            slot = cloc % 2
            pF, pF2 = (pE, pE2) if cloc % 2 == 0 else (pG, pH)
            kF, kF2 = ("pE", "pE2") if cloc % 2 == 0 else ("pG", "pH")
            gwait("gpsimd", ("xrb4", slot))
            ldn = dop("gpsimd", lambda e, bb_=tb, rr=r0, sl=slot: e.dma_start(
                xrb4[:, sl * DM:(sl + 1) * DM], x_d[bb_, rr:rr + 128, :]), x4)
            wait("tensor", vW, GVW[tj])
            gwait("tensor", kF)
            gwait("tensor", kF2)
            for mo in range(8):
                bank = pF if mo < 4 else pF2
                cc0 = (mo % 4) * 128
                op("tensor", lambda e, bk=bank, c0=cc0, g=mo, cl_=cloc: e.transpose(
                    bk[:, c0:c0 + 128],
                    ffn_fm[:, g * 512 + cl_ * 128:g * 512 + (cl_ + 1) * 128],
                    eye_s[:]), mmF if mo == 7 else None)
            wait("vector", mmF, mmF.n)
            wait("vector", vC2, GVC2[tj])
            wait("vector", x4, ldn)
            gwait("vector", "t1t2")
            t1n = op("vector", lambda e, sl=slot, cc=c: e.tensor_scalar(
                t1b[:], xrb4[:, sl * DM:(sl + 1) * DM],
                muvar[:, 2 * cc:2 * cc + 1], rstd[:, cc:cc + 1],
                op0=OP.subtract, op1=OP.mult), v7)
            wait("gpsimd", v7, t1n)
            t2n = op("gpsimd", lambda e: e.tensor_tensor(
                t2b[:], t1b[:], d1b[:], op=OP.mult), p7)
            wait("vector", p7, t2n)
            yn = op("vector", lambda e, cc=c: e.tensor_tensor(
                t1b[:], t2b[:], cs_tile[:, (cc % 4) * 1024:(cc % 4 + 1) * 1024],
                op=OP.add), v7)
            GYN[tj] = v7.n
            wait("vector", v7, yn)
            ln_ = op("vector", lambda e, cc=c: e.tensor_scalar(
                t2b[:], t1b[:], muyvar[:, 2 * cc:2 * cc + 1],
                rstdy[:, cc:cc + 1], op0=OP.subtract, op1=OP.mult), v7)
            wait("gpsimd", v7, ln_)
            gn = op("gpsimd", lambda e: e.tensor_tensor(
                t1b[:], t2b[:], gbc[:], op=OP.mult), p7)
            wait("gpsimd", p7, gn)
            an = op("gpsimd", lambda e: e.tensor_tensor(
                t2b[:], t1b[:], bbc[:], op=OP.add), p7)
            wait("gpsimd", p7, an)
            s1n = op("gpsimd", lambda e, sl=slot: e.tensor_tensor(
                t1b[:], t2b[:], xrb4[:, sl * DM:(sl + 1) * DM], op=OP.add), p7)
            gset(("xrb4", slot), (p7, s1n))
            wait("vector", p7, s1n)
            gwait("vector", ("out", slot))
            o1n = op("vector", lambda e, sl=slot, p=pF: e.tensor_tensor(
                out_sb[:, sl * DM:sl * DM + 512], t1b[:, 0:512],
                p[:, 0:512], op=OP.add), v7)
            gset(kF, (v7, o1n))
            o2n = op("vector", lambda e, sl=slot, p=pF2: e.tensor_tensor(
                out_sb[:, sl * DM + 512:(sl + 1) * DM], t1b[:, 512:1024],
                p[:, 0:512], op=OP.add), vout)
            gset(kF2, (vout, o2n))
            gset("t1t2", (v7, o1n), (vout, o2n))
            wait("gpsimd", v7, o1n)
            wait("gpsimd", vout, o2n)
            sn = dop("gpsimd", lambda e, bb_=tb, rr=r0, sl=slot: e.dma_start(
                out_d[bb_, rr:rr + 128, :], out_sb[:, sl * DM:(sl + 1) * DM]),
                sout)
            gset(("out", slot), (sout, sn))

        GVW = {}
        GVC2 = {}
        GYN = {}
        G_AT_M = {4: 0, 10: 1, 16: 2, 22: 3}
        for tj in range(ntile):
            tb = tj // 4
            for cloc in range(4 if p3sub >= 1 else 0):
                c = 4 * tj + cloc
                r0 = (c % 16) * 128
                slot = c % 2
                gwait("gpsimd", ("xrb", slot))
                ldn = dop("gpsimd", lambda e, bb_=tb, rr=r0, sl=slot: e.dma_start(
                    xrb[:, sl * DM:(sl + 1) * DM], x_d[bb_, rr:rr + 128, :]), x3)
                wait("vector", x3, ldn)
                gwait("vector", ("xn_tm", slot))
                if cloc == 0:
                    gwait("vector", "xnbt")
                tn = op("vector", lambda e, sl=slot, cc=c: e.tensor_scalar(
                    xn_tm[:, sl * DM:(sl + 1) * DM], xrb[:, sl * DM:(sl + 1) * DM],
                    muvar[:, 2 * cc:2 * cc + 1], rstd[:, cc:cc + 1],
                    op0=OP.subtract, op1=OP.mult), v1)
                gset(("xrb", slot), (v1, tn))
                wait("tensor", v1, tn)
                gwait("tensor", "pA01")
                gwait("tensor", "pAL")
                gwait("tensor", "pA2L")
                for k in range(8):
                    bank = pA if k < 4 else pA2
                    cc0 = (k % 4) * 128
                    op("tensor", lambda e, bk=bank, c0=cc0, sl=slot, kk=k: e.transpose(
                        bk[:, c0:c0 + 128],
                        xn_tm[:, sl * DM + kk * 128:sl * DM + (kk + 1) * 128],
                        eye_s[:]), mmT3 if k == 7 else None)
                gset(("xn_tm", slot), (mmT3, mmT3.n))
                wait("vector", mmT3, mmT3.n)
                for k in range(8):
                    bank = pA if k < 4 else pA2
                    cc0 = (k % 4) * 128
                    op("vector", lambda e, bk=bank, c0=cc0, kk=k, cl_=cloc: e.tensor_copy(
                        xnbt[:, kk * 512 + cl_ * 128:kk * 512 + (cl_ + 1) * 128],
                        bk[:, c0:c0 + 128]), v3 if k == 7 else None)
                gset("pA01", (v3, v3.n))
            wait("tensor", v3, v3.n)
            for m in range(32 if p3sub >= 3 else 0):
                if tj > 0 and p3sub >= 7 and m in G_AT_M:
                    emit_g(tj - 1, G_AT_M[m])
                sl1, sl2 = (2 * m) % 4, (2 * m + 1) % 4
                if tj == 0 and m == 0:
                    wait("sync", cl, 16)
                    wait("sync", clB, 16)
                gwait("sync", ("l1", sl1))
                wna = dop("sync", lambda e, mm_=m, s=sl1: e.dma_start(
                    l1slot(s), w1b_d[mm_]), wl)
                gwait("sync", ("l1", sl2))
                wn = dop("sync", lambda e, mm_=m, s=sl2: e.dma_start(
                    l1slot(s), w2b_d[mm_]), wlB)
                pa = pA if m % 2 == 0 else pA2
                pb = pB if m % 2 == 0 else pB2
                ka = "pAL" if m % 2 == 0 else "pA2L"
                kb = "pBL" if m % 2 == 0 else "pB2L"
                wait("tensor", wl, wna)
                wait("tensor", wlB, wn)
                if m == 0:
                    gwait("tensor", "pA01")
                    gwait("tensor", "pB0")
                gwait("tensor", ka)
                gwait("tensor", kb)
                for k in range(8):
                    op("tensor", lambda e, p=pa, s=sl1, kk=k: e.matmul(
                        p[:, 0:512], l1slot(s)[:, kk * 128:(kk + 1) * 128],
                        xnbt[:, kk * 512:(kk + 1) * 512],
                        start=(kk == 0), stop=(kk == 7)), None)
                for k in range(8):
                    op("tensor", lambda e, p=pb, s=sl2, kk=k: e.matmul(
                        p[:, 0:512], l1slot(s)[:, kk * 128:(kk + 1) * 128],
                        xnbt[:, kk * 512:(kk + 1) * 512],
                        start=(kk == 0), stop=(kk == 7)), mmL if k == 7 else None)
                gset(("l1", sl1), (mmL, mmL.n))
                gset(("l1", sl2), (mmL, mmL.n))
                wait("scalar", mmL, mmL.n)
                gwait("scalar", ("th", m % 2))
                gwait("scalar", "usilu_u")
                thn = op("scalar", lambda e, p=pa, g=m, sl=m % 2: e.activation(
                    thbuf[:, sl * 512:(sl + 1) * 512], p[:, 0:512], AF.Tanh,
                    scale=s1h[:, g:g + 1]), scL)
                un = op("scalar", lambda e, p=pa, g=m: e.activation(
                    ubuf[:], p[:, 0:512], AF.Copy, scale=s1h2[:, g:g + 1]), scL)
                gset(ka, (scL, un))
                wait("vector", scL, un)
                gwait("vector", "usilu")
                if m == 0:
                    gwait("vector", "hbuf_w")
                sn = op("vector", lambda e, sl=m % 2: e.scalar_tensor_tensor(
                    silubuf[:], thbuf[:, sl * 512:(sl + 1) * 512], 1.0, ubuf[:],
                    op0=OP.add, op1=OP.mult), vL)
                gset(("th", m % 2), (vL, sn))
                gset("usilu_u", (vL, sn))
                wait("vector", vL, sn)
                hn = op("vector", lambda e, p=pb, g=m: e.tensor_tensor(
                    hbuf[:, g * 512:(g + 1) * 512], silubuf[:], p[:, 0:512],
                    op=OP.mult), vL)
                gset(kb, (vL, hn))
                gset("usilu", (vL, hn))
            for cloc in range(4 if p3sub >= 2 else 0):
                c = 4 * tj + cloc
                gwait("tensor", "pE")
                gwait("tensor", "pE2")
                if cloc == 0:
                    wait("tensor", v7, GYN.get(tj - 1, 0))
                    wait("vector", v7, GYN.get(tj - 1, 0))
                sc0_ = xcol(c // 16, (c % 16) * 128)
                op("tensor", lambda e, s0=sc0_: e.matmul(
                    pE[:, 0:512], states[:, s0:s0 + 128],
                    ctb[:, 0:512], start=True, stop=True), mmC2)
                mn = op("tensor", lambda e, s0=sc0_: e.matmul(
                    pE2[:, 0:512], states[:, s0:s0 + 128],
                    ctb[:, 512:1024], start=True, stop=True), mmC2)
                wait("vector", mmC2, mn)
                op("vector", lambda e, cl_=cloc: e.tensor_copy(
                    cs_tile[:, cl_ * 1024:cl_ * 1024 + 512], pE[:, 0:512]), vC2)
                cn = op("vector", lambda e, cl_=cloc: e.tensor_copy(
                    cs_tile[:, cl_ * 1024 + 512:(cl_ + 1) * 1024], pE2[:, 0:512]), vC2)
                gset("pE", (vC2, cn))
                gset("pE2", (vC2, cn))
            L1_END_VL = vL.n
            for lay in range(2 if p3sub >= 5 else (1 if p3sub >= 4 else 0)):
                wsrc = h1b_d if lay == 0 else h2b_d
                sc_ap = h1spk if lay == 0 else h2spk
                rsrc = hbuf if lay == 0 else g1buf
                wdst = g1buf if lay == 0 else hbuf
                H1_END_SCH = scH.n
                for m in range(32):
                    hs = hring_i[0] % 2
                    hring_i[0] += 1
                    if tj == 0 and m == 0:
                        wait("sync", clC if lay == 0 else clD, 16)
                    gwait("sync", ("hr", hs))
                    wn = dop("sync", lambda e, mm_=m, s=hs, w=wsrc: e.dma_start(
                        hslot(s), w[mm_]), wh)
                    pe_ = pE if m % 2 == 0 else pE2
                    ke = "pE" if m % 2 == 0 else "pE2"
                    wait("tensor", wh, wn)
                    if lay == 0 and m == 0:
                        wait("tensor", vL, L1_END_VL)
                    if lay == 1 and m == 0:
                        wait("tensor", scH, H1_END_SCH)
                    gwait("tensor", ke)
                    for k in range(32):
                        op("tensor", lambda e, p=pe_, s=hs, kk=k, r=rsrc: e.matmul(
                            p[:, 0:512], hslot(s)[:, kk * 128:(kk + 1) * 128],
                            r[:, kk * 512:(kk + 1) * 512],
                            start=(kk == 0), stop=(kk == 31)),
                            mmH if k == 31 else None)
                    gset(("hr", hs), (mmH, mmH.n))
                    wait("scalar", mmH, mmH.n)
                    gn = op("scalar", lambda e, p=pe_, g=m, w=wdst, s=sc_ap: e.activation(
                        w[:, g * 512:(g + 1) * 512], p[:, 0:512], AF.Gelu,
                        scale=s[:, g:g + 1]), scH)
                    gset(ke, (scH, gn))
            H2_END_SCH = scH.n
            for mo in range(8 if p3sub >= 6 else 0):
                hs = hring_i[0] % 2
                hring_i[0] += 1
                if tj == 0 and mo == 0:
                    wait("sync", cv, 16)
                gwait("sync", ("hr", hs))
                wn = dop("sync", lambda e, mm_=mo, s=hs: e.dma_start(
                    hslot(s), w3b_d[mm_]), wh)
                pw = pG if mo % 2 == 0 else pH
                kw = "pG" if mo % 2 == 0 else "pH"
                wait("tensor", wh, wn)
                if mo == 0:
                    wait("tensor", scH, H2_END_SCH)
                gwait("tensor", kw)
                for k in range(32):
                    op("tensor", lambda e, p=pw, s=hs, kk=k: e.matmul(
                        p[:, 0:512], hslot(s)[:, kk * 128:(kk + 1) * 128],
                        hbuf[:, kk * 512:(kk + 1) * 512],
                        start=(kk == 0), stop=(kk == 31)), mmW if k == 31 else None)
                gset(("hr", hs), (mmW, mmW.n))
                wait("vector", mmW, mmW.n)
                if mo == 0:
                    wait("vector", mmF, mmF.n)
                en = op("vector", lambda e, p=pw, g=mo: e.tensor_scalar(
                    ffn_fm[:, g * 512:(g + 1) * 512], p[:, 0:512],
                    w3spk[:, g:g + 1], None, op0=OP.mult), vW)
                gset(kw, (vW, en))
            W3_END_VW = vW.n
            gset("hbuf_w", (mmW, mmW.n))
            GVW[tj] = vW.n
            gset("hbuf_w", (mmW, mmW.n))
            GVC2[tj] = vC2.n
            gset("xnbt", (mmL, mmL.n))
        for cloc in range(4 if (p3sub >= 7 and ntile > 0) else 0):
            emit_g(ntile - 1, cloc)
        wait("gpsimd", sout, sout.n)
        if debug:
            wait("gpsimd", sdbg, sdbg.n)

        with nc.Block("ph2") as blk2:
            materialize(blk2, ST2)

    return nc


def _tile_io(w):
    wt = np.asarray(w, dtype=np.float32).T
    v = wt.reshape(8, 128, 32, 128)
    return np.ascontiguousarray(v.transpose(2, 1, 0, 3).reshape(32, 128, 8 * 128))


def _tile_hh(w):
    wt = np.asarray(w, dtype=np.float32).T
    I, O = wt.shape
    nm = O // 128
    v = wt.reshape(I // 1024, 8, 128, nm, 128)
    return np.ascontiguousarray(v.transpose(3, 2, 0, 1, 4).reshape(nm, 128, I))


def _prep_inputs(inputs):
    f32c = lambda a: np.ascontiguousarray(np.asarray(a, dtype=np.float32))
    x = f32c(inputs["L0_attn_x"])
    common = {
        "at": f32c(np.asarray(inputs["L0_attn_A"]).T),
        "bt": f32c(np.asarray(inputs["L0_attn_B"]).T),
        "ct": f32c(np.asarray(inputs["L0_attn_C"]).T),
        "drow": f32c(np.asarray(inputs["L0_attn_D"]).reshape(1, DM)),
        "grow": f32c(np.asarray(inputs["L0_attn_gamma"]).reshape(1, DM)),
        "brow": f32c(np.asarray(inputs["L0_attn_beta"]).reshape(1, DM)),
        "w1t": _tile_io(inputs["L0_ffn_w1_w"]),
        "w2t": _tile_io(inputs["L0_ffn_w2_w"]),
        "h1t": _tile_hh(inputs["L0_ffn_h1_w"]),
        "h2t": _tile_hh(inputs["L0_ffn_h2_w"]),
        "w3t": _tile_hh(inputs["L0_ffn_w3_w"]),
        "s1pk": f32c(np.asarray(inputs["L0_ffn_w1_s"]).reshape(32, 128).T),
        "s2pk": f32c(np.asarray(inputs["L0_ffn_w2_s"]).reshape(32, 128).T),
        "h1spk": f32c(np.asarray(inputs["L0_ffn_h1_s"]).reshape(32, 128).T),
        "h2spk": f32c(np.asarray(inputs["L0_ffn_h2_s"]).reshape(32, 128).T),
        "w3spk": f32c(np.asarray(inputs["L0_ffn_w3_s"]).reshape(8, 128).T),
        "eye": np.eye(128, dtype=np.float32),
        "ones": np.ones((1, 128), dtype=np.float32),
    }
    in_maps = []
    for c in range(8):
        m = dict(common)
        m["x"] = np.ascontiguousarray(x[2 * c:2 * c + 2])
        in_maps.append(m)
    return in_maps


def kernel(**inputs):
    in_maps = _prep_inputs(inputs)
    nc = _build(debug=False)
    res = run_bass_kernel_spmd(nc, in_maps, core_ids=list(range(8)))
    return np.concatenate([r["out"] for r in res.results], axis=0)


# version 13
# speedup vs baseline: 1.0234x; 1.0178x over previous
import contextlib
import numpy as np
import ml_dtypes
import concourse.bass as bass
import concourse.mybir as mybir
from concourse.bass_utils import run_bass_kernel_spmd

F32 = mybir.dt.float32
BF16 = mybir.dt.bfloat16
AF = mybir.ActivationFunctionType
OP = mybir.AluOpType

SEQ = 2048
DM = 1024
DC = 8
HID = 4096
HC = 32
NTILE = 8
NCH = 32
EPS = 1e-5


class Sem:
    def __init__(self, h):
        self.h = h
        self.n = 0


def _build(debug=False, ntile=NTILE, ncores=8, p3sub=7, inj_start=96, inj_every=2):
    nc = bass.Bass("TRN2", target_bir_lowering=False)

    x_d = nc.dram_tensor("x", [2, SEQ, DM], F32, kind="ExternalInput").ap()
    at_d = nc.dram_tensor("at", [128, 128], F32, kind="ExternalInput").ap()
    bt_d = nc.dram_tensor("bt", [DM, 128], F32, kind="ExternalInput").ap()
    ct_d = nc.dram_tensor("ct", [128, DM], F32, kind="ExternalInput").ap()
    drow_d = nc.dram_tensor("drow", [1, DM], F32, kind="ExternalInput").ap()
    grow_d = nc.dram_tensor("grow", [1, DM], F32, kind="ExternalInput").ap()
    brow_d = nc.dram_tensor("brow", [1, DM], F32, kind="ExternalInput").ap()
    w1t_d = nc.dram_tensor("w1t", [32, 128, 1024], F32, kind="ExternalInput").ap()
    w2t_d = nc.dram_tensor("w2t", [32, 128, 1024], F32, kind="ExternalInput").ap()
    h1t_d = nc.dram_tensor("h1t", [32, 128, 4096], F32, kind="ExternalInput").ap()
    h2t_d = nc.dram_tensor("h2t", [32, 128, 4096], F32, kind="ExternalInput").ap()
    w3t_d = nc.dram_tensor("w3t", [8, 128, 4096], F32, kind="ExternalInput").ap()
    s1_d = nc.dram_tensor("s1pk", [128, 32], F32, kind="ExternalInput").ap()
    s2_d = nc.dram_tensor("s2pk", [128, 32], F32, kind="ExternalInput").ap()
    h1s_d = nc.dram_tensor("h1spk", [128, 32], F32, kind="ExternalInput").ap()
    h2s_d = nc.dram_tensor("h2spk", [128, 32], F32, kind="ExternalInput").ap()
    w3s_d = nc.dram_tensor("w3spk", [128, 8], F32, kind="ExternalInput").ap()
    eye_d = nc.dram_tensor("eye", [128, 128], F32, kind="ExternalInput").ap()
    ones_d = nc.dram_tensor("ones", [1, 128], F32, kind="ExternalInput").ap()
    out_d = nc.dram_tensor("out", [2, SEQ, DM], F32, kind="ExternalOutput").ap()

    w1b_d = nc.dram_tensor("w1b", [32, 128, 1024], BF16, kind="Internal").ap()
    w2b_d = nc.dram_tensor("w2b", [32, 128, 1024], BF16, kind="Internal").ap()
    h1b_d = nc.dram_tensor("h1b", [32, 128, 4096], BF16, kind="Internal").ap()
    h2b_d = nc.dram_tensor("h2b", [32, 128, 4096], BF16, kind="Internal").ap()
    w3b_d = nc.dram_tensor("w3b", [8, 128, 4096], BF16, kind="Internal").ap()

    if debug:
        dbg_xb = nc.dram_tensor("dbg_xb", [128, 4352], F32, kind="ExternalOutput").ap()
        dbg_st = nc.dram_tensor("dbg_st", [128, 4352], BF16, kind="ExternalOutput").ap()
        dbg_mv = nc.dram_tensor("dbg_mv", [128, 64], F32, kind="ExternalOutput").ap()
        dbg_rs = nc.dram_tensor("dbg_rs", [128, 32], F32, kind="ExternalOutput").ap()
        dbg_myv = nc.dram_tensor("dbg_myv", [128, 64], F32, kind="ExternalOutput").ap()
        dbg_rsy = nc.dram_tensor("dbg_rsy", [128, 32], F32, kind="ExternalOutput").ap()

    ENGS = ("sync", "tensor", "vector", "scalar", "gpsimd")
    ST1 = {e: [] for e in ENGS}
    ST2 = {e: [] for e in ENGS}
    CUR = [ST1]
    WAITED = {e: {} for e in ENGS}
    GUARD = {}

    def em(eng, fn):
        CUR[0][eng].append(fn)

    def wait(eng, sem, val):
        if val <= 0:
            return
        if WAITED[eng].get(sem, 0) >= val:
            return
        WAITED[eng][sem] = val
        em(eng, lambda e, h=sem.h, v=val: e.wait_ge(h, v))

    def op(eng, fn, sem=None, k=1):
        if sem is not None:
            sem.n += k
            em(eng, lambda e, f=fn, h=sem.h, kk=k: f(e).then_inc(h, kk))
            return sem.n
        em(eng, fn)
        return None

    def dop(eng, fn, sem):
        wait(eng, sem, sem.n)
        return op(eng, fn, sem, 16)

    def gwait(eng, key):
        for s, v in GUARD.get(key, ()):
            wait(eng, s, v)

    def gset(key, *pairs):
        GUARD[key] = list(pairs)

    def materialize(blk, streams):
        @blk.sync
        def _(e):
            for f in streams["sync"]:
                f(e)

        @blk.tensor
        def _(e):
            for f in streams["tensor"]:
                f(e)

        @blk.vector
        def _(e):
            for f in streams["vector"]:
                f(e)

        @blk.scalar
        def _(e):
            for f in streams["scalar"]:
                f(e)

        @blk.gpsimd
        def _(e):
            for f in streams["gpsimd"]:
                f(e)

    es = contextlib.ExitStack()
    with es:
        names = ("cld", "cv0", "cmm", "x0", "v0", "sc0", "v0b", "x1",
                 "v1", "mmT", "v2", "mmB", "xb", "ac", "smm",
                 "cl", "clB", "clC", "clD", "cv", "cs", "csB", "csC", "csD", "x2", "mmC", "v5g", "v5",
                 "sc2", "v5b", "wl", "wlB", "wh", "mmL", "scL", "vL",
                 "mmH", "scH", "mmW", "vW", "mmC2", "vC2", "mmT3",
                 "v3", "x3", "x4", "mmF", "v7", "vout", "sout", "sdbg", "vsc", "p7")
        SM = {n: Sem(es.enter_context(nc.semaphore(n))) for n in names}
        (cld, cv0, cmm, x0, v0, sc0, v0b, x1, v1, mmT, v2, mmB, xb, ac,
         smm, cl, clB, clC, clD, cv, cs, csB, csC, csD, x2, mmC, v5g, v5,
         sc2, v5b, wl, wlB, wh, mmL, scL, vL, mmH, scH, mmW, vW, mmC2,
         vC2, mmT3, v3, x3, x4, mmF, v7, vout, sout, sdbg, vsc, p7) = (
            SM[n] for n in names)

        sbT = lambda n, shp, dt: es.enter_context(nc.sbuf_tensor(n, shp, dt))
        states = sbT("states", [128, 4352], BF16)
        xcol = lambda b, t: b * 2176 + 64 + t
        xn_tm = sbT("xn_tm", [128, 2 * DM], F32)
        xrb = sbT("xrb", [128, 2 * DM], F32)
        t1b = sbT("t1b", [128, DM], F32)
        t2b = sbT("t2b", [128, DM], F32)
        d1b = sbT("d1b", [128, DM], F32)
        gbc = sbT("gbc", [128, DM], F32)
        bbc = sbT("bbc", [128, DM], F32)
        ctb = sbT("ctb", [128, DM], BF16)
        eye_s = sbT("eye_s", [128, 128], F32)
        rowbuf = sbT("rowbuf", [1, DM], F32)
        ones_s = sbT("ones_s", [1, 128], F32)
        eps_s = sbT("eps_s", [128, 1], F32)
        s1pk = sbT("s1pk_s", [128, 32], F32)
        s2pk = sbT("s2pk_s", [128, 32], F32)
        h1spk = sbT("h1spk_s", [128, 32], F32)
        h2spk = sbT("h2spk_s", [128, 32], F32)
        w3spk = sbT("w3spk_s", [128, 8], F32)
        s1h = sbT("s1h", [128, 32], F32)
        s1h2 = sbT("s1h2", [128, 32], F32)
        muvar = sbT("muvar", [128, 64], F32)
        rstd = sbT("rstd", [128, 32], F32)
        sqtmp = sbT("sqtmp", [128, 32], F32)
        muyvar = sbT("muyvar", [128, 64], F32)
        rstdy = sbT("rstdy", [128, 32], F32)
        bnb = sbT("bnb", [128, 12], F32)
        xnbt = sbT("xnbt", [128, DC * 512], BF16)
        hbuf = sbT("hbuf", [128, HC * 512], BF16)
        g1buf = sbT("g1buf", [128, HC * 512], BF16)
        hring = sbT("hring", [128, 2 * 4096], BF16)
        l1ring = sbT("l1ring", [128, 4 * 1024], BF16)
        thbuf = sbT("thbuf", [128, 2 * 512], F32)
        ubuf = sbT("ubuf", [128, 512], F32)
        silubuf = sbT("silubuf", [128, 512], F32)
        hslot = lambda s: hring[:, s * 4096:(s + 1) * 4096]
        l1slot = lambda s: l1ring[:, s * 1024:(s + 1) * 1024]
        PH2 = {}

        psT = lambda n: es.enter_context(nc.psum_tensor(n, [128, 512], F32))
        pA, pA2, pB, pB2, pE, pE2, pG, pH = (psT(n) for n in
            ("pA", "pA2", "pB", "pB2", "pE", "pE2", "pG", "pH"))

        ph1 = contextlib.ExitStack()
        sb1 = lambda n, shp, dt: ph1.enter_context(nc.sbuf_tensor(n, shp, dt))
        xB = sb1("xB", [128, 4352], F32)
        sctmp = sb1("sctmp", [128, 16], F32)
        atb = sb1("atb", [128, 128], BF16)
        btb = sb1("btb", [128, DM], BF16)
        xnbw = sb1("xnbw", [128, DC * 128], BF16)
        cin = sb1("cin", [128, 2304], F32)
        cstmp = sb1("cstmp", [128, DM], BF16)

        dop("sync", lambda e: e.dma_start(cin[:, 0:128], at_d), cld)
        bt3 = bt_d.rearrange("(k p) n -> p k n", p=128)
        cin_bt = cin[:, 128:1152].rearrange("p (k n) -> p k n", n=128)
        dop("sync", lambda e: e.dma_start(cin_bt, bt3), cld)
        dop("sync", lambda e: e.dma_start(cin[:, 1152:2176], ct_d), cld)
        dop("sync", lambda e: e.dma_start(eye_s[:], eye_d), cld)
        dop("sync", lambda e: e.dma_start(ones_s[:], ones_d), cld)
        for dd, ss in ((s1_d, s1pk), (s2_d, s2pk), (h1s_d, h1spk),
                       (h2s_d, h2spk), (w3s_d, w3spk)):
            dop("sync", lambda e, d=dd, s=ss: e.dma_start(s[:], d), cld)
        wait("vector", cld, 160)
        op("vector", lambda e: e.memset(eps_s[:], EPS), cv0)
        op("vector", lambda e: e.memset(xB[:], 0.0), cv0)
        op("vector", lambda e: e.memset(states[:], 0.0), cv0)
        op("vector", lambda e: e.tensor_copy(atb[:], cin[:, 0:128]), cv0)
        op("vector", lambda e: e.tensor_copy(btb[:], cin[:, 128:1152]), cv0)
        op("vector", lambda e: e.tensor_copy(ctb[:], cin[:, 1152:2176]), cv0)
        op("vector", lambda e: e.tensor_scalar(
            s1h[:], s1pk[:], 0.5, None, op0=OP.mult), cv0)
        wait("vector", cv0, cv0.n)
        op("vector", lambda e: e.tensor_tensor(
            s1h2[:], s1h[:], s2pk[:], op=OP.mult), cv0)
        CV0_CONSTS = cv0.n

        for rd, dst, plus1 in ((drow_d, d1b, True), (grow_d, gbc, False),
                               (brow_d, bbc, False)):
            gwait("sync", "rowbuf")
            ldn = dop("sync", lambda e, d=rd: e.dma_start(rowbuf[:], d), cld)
            for hh in range(2):
                wait("tensor", cld, ldn)
                gwait("tensor", "pG")
                mn = op("tensor", lambda e, h=hh: e.matmul(
                    pG[:, 0:512], ones_s[0:1, :],
                    rowbuf[0:1, h * 512:(h + 1) * 512],
                    start=True, stop=True), cmm)
                wait("vector", cmm, mn)
                if plus1:
                    en = op("vector", lambda e, h=hh, ds=dst: e.tensor_scalar(
                        ds[:, h * 512:(h + 1) * 512], pG[:, 0:512],
                        1.0, None, op0=OP.add), cv0)
                else:
                    en = op("vector", lambda e, h=hh, ds=dst: e.tensor_copy(
                        ds[:, h * 512:(h + 1) * 512], pG[:, 0:512]), cv0)
                gset("pG", (cv0, en))
            gset("rowbuf", (cmm, cmm.n))
        CV0_ALL = cv0.n

        for i in range(32):
            j, b = i // 2, i % 2
            c = b * 16 + j
            slot = i % 2
            gwait("gpsimd", ("xrb", slot))
            ldn = dop("gpsimd", lambda e, bb_=b, r0=j * 128, sl=slot: e.dma_start(
                xrb[:, sl * DM:(sl + 1) * DM], x_d[bb_, r0:r0 + 128, :]), x0)
            wait("vector", x0, ldn)
            gwait("vector", "bnb")
            op("vector", lambda e, sl=slot: e.bn_stats(
                bnb[:, 0:6], xrb[:, sl * DM:sl * DM + 512]), v0)
            sn2 = op("vector", lambda e, sl=slot: e.bn_stats(
                bnb[:, 6:12], xrb[:, sl * DM + 512:(sl + 1) * DM]), v0)
            wait("vector", v0, sn2)
            an = op("vector", lambda e, cc=c: e.bn_aggr(
                muvar[:, 2 * cc:2 * cc + 2], bnb[:]), v0)
            gset("bnb", (v0, an))
            wait("scalar", v0, an)
            wait("scalar", cv0, 1)
            gwait("scalar", ("sqt", slot))
            qn = op("scalar", lambda e, cc=c, sl=slot: e.activation(
                sqtmp[:, sl:sl + 1], muvar[:, 2 * cc + 1:2 * cc + 2], AF.Sqrt,
                bias=eps_s[:]), sc0)
            wait("vector", sc0, qn)
            rn = op("vector", lambda e, cc=c, sl=slot: e.reciprocal(
                rstd[:, cc:cc + 1], sqtmp[:, sl:sl + 1]), v0b)
            gset(("sqt", slot), (v0b, rn))
            wait("vector", v0b, rn)
            gwait("vector", ("xn_tm", slot))
            tn = op("vector", lambda e, sl=slot, cc=c: e.tensor_scalar(
                xn_tm[:, sl * DM:(sl + 1) * DM], xrb[:, sl * DM:(sl + 1) * DM],
                muvar[:, 2 * cc:2 * cc + 1], rstd[:, cc:cc + 1],
                op0=OP.subtract, op1=OP.mult), v1)
            gset(("xrb", slot), (v1, tn))
            wait("tensor", v1, tn)
            if i == 0:
                wait("tensor", cld, 160)
                wait("tensor", cv0, CV0_CONSTS)
            gwait("tensor", "pA01")
            for k in range(8):
                bank = pA if k < 4 else pA2
                cc0 = (k % 4) * 128
                op("tensor", lambda e, bk=bank, c0=cc0, sl=slot, kk=k: e.transpose(
                    bk[:, c0:c0 + 128],
                    xn_tm[:, sl * DM + kk * 128:sl * DM + (kk + 1) * 128],
                    eye_s[:]), mmT if k == 7 else None)
            gset(("xn_tm", slot), (mmT, mmT.n))
            wait("vector", mmT, mmT.n)
            gwait("vector", "xnbw")
            for k in range(8):
                bank = pA if k < 4 else pA2
                cc0 = (k % 4) * 128
                op("vector", lambda e, bk=bank, c0=cc0, kk=k: e.tensor_copy(
                    xnbw[:, kk * 128:(kk + 1) * 128], bk[:, c0:c0 + 128]),
                    v2 if k == 7 else None)
            gset("pA01", (v2, v2.n))
            wait("tensor", v2, v2.n)
            gwait("tensor", "pB0")
            for k in range(8):
                op("tensor", lambda e, kk=k: e.matmul(
                    pB[:, 0:128], btb[:, kk * 128:(kk + 1) * 128],
                    xnbw[:, kk * 128:(kk + 1) * 128],
                    start=(kk == 0), stop=(kk == 7)), mmB if k == 7 else None)
            gset("xnbw", (mmB, mmB.n))
            wait("vector", mmB, mmB.n)
            op("vector", lambda e, b_=b, jj=j: e.tensor_copy(
                xB[:, xcol(b_, jj * 128):xcol(b_, jj * 128) + 128], pB[:, 0:128]), xb)
            gset("pB0", (xb, xb.n))

        gset("sqtmp", (v0b, v0b.n))
        CSEM = {"w1": cl, "w2": clB, "h1": clC, "h2": clD, "w3": cv}
        for nm_, s_, d_ in (("w1", w1t_d, w1b_d), ("w2", w2t_d, w2b_d),
                            ("h1", h1t_d, h1b_d), ("h2", h2t_d, h2b_d),
                            ("w3", w3t_d, w3b_d)):
            op("gpsimd", lambda e, ss=s_, dd=d_: e.dma_start(dd, ss), CSEM[nm_], 16)


        hring_i = [0]
        GVW, GVC2, GYN, LEND, H1END, H2END = {}, {}, {}, {}, {}, {}

        def emit_g(tj, cloc):
            tb_ = tj // 4
            c = 4 * tj + cloc
            r0 = (c % 16) * 128
            slot = cloc % 2
            pF, pF2 = (pE, pE2) if cloc % 2 == 0 else (pG, pH)
            kF, kF2 = ("pE", "pE2") if cloc % 2 == 0 else ("pG", "pH")
            gwait("gpsimd", ("xrb4", slot))
            ldn = dop("gpsimd", lambda e, bb_=tb_, rr=r0, sl=slot: e.dma_start(
                xrb4[:, sl * DM:(sl + 1) * DM], x_d[bb_, rr:rr + 128, :]), x4)
            wait("tensor", vW, GVW[tj])
            gwait("tensor", kF)
            gwait("tensor", kF2)
            for mo in range(8):
                bank = pF if mo < 4 else pF2
                cc0 = (mo % 4) * 128
                op("tensor", lambda e, bk=bank, c0=cc0, g=mo, cl_=cloc: e.transpose(
                    bk[:, c0:c0 + 128],
                    ffn_fm[:, g * 512 + cl_ * 128:g * 512 + (cl_ + 1) * 128],
                    eye_s[:]), mmF if mo == 7 else None)
            wait("vector", mmF, mmF.n)
            wait("vector", vC2, GVC2[tj])
            wait("vector", x4, ldn)
            gwait("vector", "t1t2")
            t1n = op("vector", lambda e, sl=slot, cc=c: e.tensor_scalar(
                t1b[:], xrb4[:, sl * DM:(sl + 1) * DM],
                muvar[:, 2 * cc:2 * cc + 1], rstd[:, cc:cc + 1],
                op0=OP.subtract, op1=OP.mult), v7)
            wait("gpsimd", v7, t1n)
            t2n = op("gpsimd", lambda e: e.tensor_tensor(
                t2b[:], t1b[:], d1b[:], op=OP.mult), p7)
            wait("vector", p7, t2n)
            yn = op("vector", lambda e, cl_=cloc: e.tensor_tensor(
                t1b[:], t2b[:], cs_tile[:, cl_ * 1024:(cl_ + 1) * 1024],
                op=OP.add), v7)
            GYN[tj] = v7.n
            wait("vector", v7, yn)
            ln_ = op("vector", lambda e, cc=c: e.tensor_scalar(
                t2b[:], t1b[:], muyvar[:, 2 * cc:2 * cc + 1],
                rstdy[:, cc:cc + 1], op0=OP.subtract, op1=OP.mult), v7)
            wait("gpsimd", v7, ln_)
            gn = op("gpsimd", lambda e: e.tensor_tensor(
                t1b[:], t2b[:], gbc[:], op=OP.mult), p7)
            wait("gpsimd", p7, gn)
            an = op("gpsimd", lambda e: e.tensor_tensor(
                t2b[:], t1b[:], bbc[:], op=OP.add), p7)
            wait("gpsimd", p7, an)
            s1n = op("gpsimd", lambda e, sl=slot: e.tensor_tensor(
                t1b[:], t2b[:], xrb4[:, sl * DM:(sl + 1) * DM], op=OP.add), p7)
            gset(("xrb4", slot), (p7, s1n))
            wait("vector", p7, s1n)
            gwait("vector", ("out", slot))
            o1n = op("vector", lambda e, sl=slot, p=pF: e.tensor_tensor(
                out_sb[:, sl * DM:sl * DM + 512], t1b[:, 0:512],
                p[:, 0:512], op=OP.add), v7)
            gset(kF, (v7, o1n))
            o2n = op("vector", lambda e, sl=slot, p=pF2: e.tensor_tensor(
                out_sb[:, sl * DM + 512:(sl + 1) * DM], t1b[:, 512:1024],
                p[:, 0:512], op=OP.add), vout)
            gset(kF2, (vout, o2n))
            gset("t1t2", (v7, o1n), (vout, o2n))
            wait("gpsimd", v7, o1n)
            wait("gpsimd", vout, o2n)
            sn = dop("gpsimd", lambda e, bb_=tb_, rr=r0, sl=slot: e.dma_start(
                out_d[bb_, rr:rr + 128, :], out_sb[:, sl * DM:(sl + 1) * DM]),
                sout)
            gset(("out", slot), (sout, sn))

        def gen_a(tj):
            tb_ = tj // 4
            for cloc in range(4):
                c = 4 * tj + cloc
                r0 = (c % 16) * 128
                slot = c % 2
                gwait("gpsimd", ("xrb", slot))
                ldn = dop("gpsimd", lambda e, bb_=tb_, rr=r0, sl=slot: e.dma_start(
                    xrb[:, sl * DM:(sl + 1) * DM], x_d[bb_, rr:rr + 128, :]), x3)
                wait("vector", x3, ldn)
                gwait("vector", ("xn_tm", slot))
                if cloc == 0:
                    gwait("vector", "xnbt")
                tn = op("vector", lambda e, sl=slot, cc=c: e.tensor_scalar(
                    xn_tm[:, sl * DM:(sl + 1) * DM], xrb[:, sl * DM:(sl + 1) * DM],
                    muvar[:, 2 * cc:2 * cc + 1], rstd[:, cc:cc + 1],
                    op0=OP.subtract, op1=OP.mult), v1)
                gset(("xrb", slot), (v1, tn))
                wait("tensor", v1, tn)
                gwait("tensor", "pA01")
                gwait("tensor", "pAL")
                gwait("tensor", "pA2L")
                for k in range(8):
                    bank = pA if k < 4 else pA2
                    cc0 = (k % 4) * 128
                    op("tensor", lambda e, bk=bank, c0=cc0, sl=slot, kk=k: e.transpose(
                        bk[:, c0:c0 + 128],
                        xn_tm[:, sl * DM + kk * 128:sl * DM + (kk + 1) * 128],
                        eye_s[:]), mmT3 if k == 7 else None)
                    if k == 3:
                        yield
                gset(("xn_tm", slot), (mmT3, mmT3.n))
                wait("vector", mmT3, mmT3.n)
                for k in range(8):
                    bank = pA if k < 4 else pA2
                    cc0 = (k % 4) * 128
                    op("vector", lambda e, bk=bank, c0=cc0, kk=k, cl_=cloc: e.tensor_copy(
                        xnbt[:, kk * 512 + cl_ * 128:kk * 512 + (cl_ + 1) * 128],
                        bk[:, c0:c0 + 128]), v3 if k == 7 else None)
                gset("pA01", (v3, v3.n))
                yield

        def gen_l1(tj):
            wait("tensor", v3, v3.n)
            pend = []

            def epi(m):
                pa = pA if m % 2 == 0 else pA2
                pb = pB if m % 2 == 0 else pB2
                ka = "pAL" if m % 2 == 0 else "pA2L"
                kb = "pBL" if m % 2 == 0 else "pB2L"
                wait("scalar", mmL, MML[m])
                gwait("scalar", ("th", m % 2))
                gwait("scalar", "usilu_u")
                thn = op("scalar", lambda e, p=pa, g=m, sl=m % 2: e.activation(
                    thbuf[:, sl * 512:(sl + 1) * 512], p[:, 0:512], AF.Tanh,
                    scale=s1h[:, g:g + 1]), scL)
                un = op("scalar", lambda e, p=pa, g=m: e.activation(
                    ubuf[:], p[:, 0:512], AF.Copy, scale=s1h2[:, g:g + 1]), scL)
                gset(ka, (scL, un))
                wait("vector", scL, un)
                gwait("vector", "usilu")
                if tj > 0 and m == 0:
                    gwait("vector", "hbuf_w")
                sn = op("vector", lambda e, sl=m % 2: e.scalar_tensor_tensor(
                    silubuf[:], thbuf[:, sl * 512:(sl + 1) * 512], 1.0, ubuf[:],
                    op0=OP.add, op1=OP.mult), vL)
                gset(("th", m % 2), (vL, sn))
                gset("usilu_u", (vL, sn))
                wait("vector", vL, sn)
                hn = op("vector", lambda e, p=pb, g=m: e.tensor_tensor(
                    hbuf[:, g * 512:(g + 1) * 512], silubuf[:], p[:, 0:512],
                    op=OP.mult), vL)
                gset(kb, (vL, hn))
                gset("usilu", (vL, hn))

            MML = {}
            for m in range(32):
                sl1, sl2 = (2 * m) % 4, (2 * m + 1) % 4
                if tj == 0 and m == 0:
                    wait("sync", cl, 16)
                    wait("sync", clB, 16)
                gwait("sync", ("l1", sl1))
                wna = dop("sync", lambda e, mm_=m, s=sl1: e.dma_start(
                    l1slot(s), w1b_d[mm_]), wl)
                gwait("sync", ("l1", sl2))
                wn = dop("sync", lambda e, mm_=m, s=sl2: e.dma_start(
                    l1slot(s), w2b_d[mm_]), wlB)
                pa = pA if m % 2 == 0 else pA2
                pb = pB if m % 2 == 0 else pB2
                ka = "pAL" if m % 2 == 0 else "pA2L"
                kb = "pBL" if m % 2 == 0 else "pB2L"
                wait("tensor", wl, wna)
                wait("tensor", wlB, wn)
                if m == 0:
                    gwait("tensor", "pA01")
                    gwait("tensor", "pB0")
                gwait("tensor", ka)
                gwait("tensor", kb)
                for k in range(8):
                    op("tensor", lambda e, p=pa, s=sl1, kk=k: e.matmul(
                        p[:, 0:512], l1slot(s)[:, kk * 128:(kk + 1) * 128],
                        xnbt[:, kk * 512:(kk + 1) * 512],
                        start=(kk == 0), stop=(kk == 7)), None)
                yield
                for k in range(8):
                    op("tensor", lambda e, p=pb, s=sl2, kk=k: e.matmul(
                        p[:, 0:512], l1slot(s)[:, kk * 128:(kk + 1) * 128],
                        xnbt[:, kk * 512:(kk + 1) * 512],
                        start=(kk == 0), stop=(kk == 7)), mmL if k == 7 else None)
                MML[m] = mmL.n
                gset(("l1", sl1), (mmL, mmL.n))
                gset(("l1", sl2), (mmL, mmL.n))
                pend.append(m)
                if len(pend) >= 2:
                    epi(pend.pop(0))
                yield
            while pend:
                epi(pend.pop(0))
            LEND[tj] = vL.n
            gset("xnbt", (mmL, mmL.n))

        def gen_h(tj, lay):
            wsrc = h1b_d if lay == 0 else h2b_d
            sc_ap = h1spk if lay == 0 else h2spk
            rsrc = hbuf if lay == 0 else g1buf
            wdst = g1buf if lay == 0 else hbuf
            pend = []
            MMH = {}

            def gelu(m):
                pg_ = pG if m % 2 == 0 else pH
                kg = "pG" if m % 2 == 0 else "pH"
                wait("scalar", mmH, MMH[m])
                gn = op("scalar", lambda e, p=pg_, g=m, w=wdst, s=sc_ap: e.activation(
                    w[:, g * 512:(g + 1) * 512], p[:, 0:512], AF.Gelu,
                    scale=s[:, g:g + 1]), scH)
                gset(kg, (scH, gn))

            for m in range(32):
                hs = hring_i[0] % 2
                hring_i[0] += 1
                if tj == 0 and m == 0:
                    wait("sync", clC if lay == 0 else clD, 16)
                gwait("sync", ("hr", hs))
                wn = dop("sync", lambda e, mm_=m, s=hs, w=wsrc: e.dma_start(
                    hslot(s), w[mm_]), wh)
                pg_ = pG if m % 2 == 0 else pH
                kg = "pG" if m % 2 == 0 else "pH"
                wait("tensor", wh, wn)
                if lay == 0 and m == 0:
                    wait("tensor", vL, LEND[tj])
                if lay == 1 and m == 0:
                    wait("tensor", scH, H1END[tj])
                gwait("tensor", kg)
                for k in range(32):
                    op("tensor", lambda e, p=pg_, s=hs, kk=k, r=rsrc: e.matmul(
                        p[:, 0:512], hslot(s)[:, kk * 128:(kk + 1) * 128],
                        r[:, kk * 512:(kk + 1) * 512],
                        start=(kk == 0), stop=(kk == 31)),
                        mmH if k == 31 else None)
                    if k % 8 == 7:
                        if k == 31:
                            MMH[m] = mmH.n
                            gset(("hr", hs), (mmH, mmH.n))
                            pend.append(m)
                            if len(pend) >= 2:
                                gelu(pend.pop(0))
                        yield
            while pend:
                gelu(pend.pop(0))
            if lay == 0:
                H1END[tj] = scH.n
            else:
                H2END[tj] = scH.n

        def emit_b(tj):
            for cloc in range(4):
                c = 4 * tj + cloc
                gwait("tensor", "pE")
                gwait("tensor", "pE2")
                if cloc == 0:
                    wait("tensor", v7, GYN.get(tj - 1, 0))
                    wait("vector", v7, GYN.get(tj - 1, 0))
                sc0_ = xcol(c // 16, (c % 16) * 128)
                op("tensor", lambda e, s0=sc0_: e.matmul(
                    pE[:, 0:512], states[:, s0:s0 + 128],
                    ctb[:, 0:512], start=True, stop=True), mmC2)
                mn = op("tensor", lambda e, s0=sc0_: e.matmul(
                    pE2[:, 0:512], states[:, s0:s0 + 128],
                    ctb[:, 512:1024], start=True, stop=True), mmC2)
                wait("vector", mmC2, mn)
                op("vector", lambda e, cl_=cloc: e.tensor_copy(
                    cs_tile[:, cl_ * 1024:cl_ * 1024 + 512], pE[:, 0:512]), vC2)
                cn = op("vector", lambda e, cl_=cloc: e.tensor_copy(
                    cs_tile[:, cl_ * 1024 + 512:(cl_ + 1) * 1024], pE2[:, 0:512]), vC2)
                gset("pE", (vC2, cn))
                gset("pE2", (vC2, cn))
            GVC2[tj] = vC2.n

        def emit_w3(tj):
            for mo in range(8):
                hs = hring_i[0] % 2
                hring_i[0] += 1
                if tj == 0 and mo == 0:
                    wait("sync", cv, 16)
                gwait("sync", ("hr", hs))
                wn = dop("sync", lambda e, mm_=mo, s=hs: e.dma_start(
                    hslot(s), w3b_d[mm_]), wh)
                pw = pG if mo % 2 == 0 else pH
                kw = "pG" if mo % 2 == 0 else "pH"
                wait("tensor", wh, wn)
                if mo == 0:
                    wait("tensor", scH, H2END[tj])
                gwait("tensor", kw)
                for k in range(32):
                    op("tensor", lambda e, p=pw, s=hs, kk=k: e.matmul(
                        p[:, 0:512], hslot(s)[:, kk * 128:(kk + 1) * 128],
                        hbuf[:, kk * 512:(kk + 1) * 512],
                        start=(kk == 0), stop=(kk == 31)), mmW if k == 31 else None)
                gset(("hr", hs), (mmW, mmW.n))
                wait("vector", mmW, mmW.n)
                if mo == 0:
                    wait("vector", mmF, mmF.n)
                en = op("vector", lambda e, p=pw, g=mo: e.tensor_scalar(
                    ffn_fm[:, g * 512:(g + 1) * 512], p[:, 0:512],
                    w3spk[:, g:g + 1], None, op0=OP.mult), vW)
                gset(kw, (vW, en))
            GVW[tj] = vW.n
            gset("hbuf_w", (mmW, mmW.n))

        def prefix_gen():
            if ntile == 0 or p3sub < 1:
                return
            yield from gen_a(0)
            if p3sub >= 3:
                yield from gen_l1(0)
            if p3sub >= 4:
                yield from gen_h(0, 0)
            if p3sub >= 5:
                yield from gen_h(0, 1)
        PFX = prefix_gen()

        def emit_x2(j):
            for b in (0, 1):
                slot = b
                gwait("scalar", ("xrb", slot))
                ldn = dop("scalar", lambda e, bb_=b, r0=j * 128, sl=slot: e.dma_start(
                    xrb[:, sl * DM:(sl + 1) * DM], x_d[bb_, r0:r0 + 128, :]), x2)
                X2N[(j, b)] = ldn

        def emit_p2(j):
            for b in (0, 1):
                c = b * 16 + j
                slot = b
                wait("tensor", ac, NBURN + (j % 4) * 128 + 128)
                gwait("tensor", "pG")
                gwait("tensor", "pH")
                sc0_ = xcol(b, (c % 16) * 128)
                op("tensor", lambda e, s0=sc0_: e.matmul(
                    pG[:, 0:512], states[:, s0:s0 + 128],
                    ctb[:, 0:512], start=True, stop=True), mmC)
                mn = op("tensor", lambda e, s0=sc0_: e.matmul(
                    pH[:, 0:512], states[:, s0:s0 + 128],
                    ctb[:, 512:1024], start=True, stop=True), mmC)
                wait("vector", mmC, mn)
                gwait("vector", "cstmp")
                op("vector", lambda e: e.tensor_copy(cstmp[:, 0:512], pG[:, 0:512]), v5g)
                gn = op("vector", lambda e: e.tensor_copy(
                    cstmp[:, 512:1024], pH[:, 0:512]), v5g)
                gset("pG", (v5g, gn))
                gset("pH", (v5g, gn))
                wait("vector", x2, X2N[(j, b)])
                wait("vector", cv0, CV0_ALL)
                gwait("vector", "t1t2")
                t1n = op("vector", lambda e, sl=slot, cc=c: e.tensor_scalar(
                    t1b[:], xrb[:, sl * DM:(sl + 1) * DM],
                    muvar[:, 2 * cc:2 * cc + 1], rstd[:, cc:cc + 1],
                    op0=OP.subtract, op1=OP.mult), v5)
                gset(("xrb", slot), (v5, t1n))
                wait("vector", v5, t1n)
                t2n = op("vector", lambda e: e.tensor_tensor(
                    t2b[:], t1b[:], d1b[:], op=OP.mult), v5)
                wait("vector", v5, t2n)
                wait("vector", v5g, gn)
                yn = op("vector", lambda e: e.tensor_tensor(
                    t1b[:], t2b[:], cstmp[:], op=OP.add), v5)
                gset("cstmp", (v5, yn))
                wait("vector", v5, yn)
                gwait("vector", "bnb")
                op("vector", lambda e: e.bn_stats(bnb[:, 0:6], t1b[:, 0:512]), v5)
                s2n = op("vector", lambda e: e.bn_stats(
                    bnb[:, 6:12], t1b[:, 512:1024]), v5)
                wait("vector", v5, s2n)
                an = op("vector", lambda e, cc=c: e.bn_aggr(
                    muyvar[:, 2 * cc:2 * cc + 2], bnb[:]), v5)
                gset("bnb", (v5, an))
                gset("t1t2", (v5, an))

        X2N = {}
        NBURN = 48
        st_h = states.tensor if hasattr(states, "tensor") else states
        xb_h = xB.tensor if hasattr(xB, "tensor") else xB
        def stv(off):
            return bass.AP(st_h, off, [[4352, 128], [2176, 2], [512, 4]])
        def xbv(off):
            return bass.AP(xb_h, off, [[4352, 128], [2176, 2], [512, 4]])
        wait("tensor", cv0, CV0_CONSTS)
        wait("vector", xb, 32)
        wait("scalar", xb, 32)
        for n in range(NBURN + 512):
            i = n - NBURN
            bank = pE if n % 2 == 0 else pE2
            s8 = (n % 2) * 8
            wait("tensor", ac, n)
            mn = op("tensor", lambda e, bk=bank, off=63 + i: e.matmul(
                bk[:, 0:8], atb[:], stv(off), start=True, stop=True), smm)
            wait("vector", smm, mn)
            an = op("vector", lambda e, bk=bank, ss=s8, off=64 + i: e.tensor_tensor(
                sctmp[:, ss:ss + 8], bk[:, 0:8], xbv(off), op=OP.add), vsc)
            wait("scalar", vsc, an)
            op("scalar", lambda e, ss=s8, off=64 + i: e.activation(
                stv(off), sctmp[:, ss:ss + 8], AF.Tanh), ac)
            if n >= inj_start and (n - inj_start) % inj_every == 0:
                next(PFX, None)
        for _ in PFX:
            pass
        gset("pE", (vsc, vsc.n))
        gset("pE2", (vsc, vsc.n))
        for j in range(16):
            emit_x2(j)
            emit_p2(j)
        gset("pE", (ac, ac.n))
        gset("pE2", (ac, ac.n))

        wait("scalar", v5, v5.n)
        gwait("scalar", "sqtmp")
        qn = op("scalar", lambda e: e.activation(
            sqtmp[:], muyvar[:, 1:64:2], AF.Sqrt, bias=eps_s[:]), sc2)
        wait("vector", sc2, qn)
        op("vector", lambda e: e.reciprocal(rstdy[:], sqtmp[:]), v5b)

        if debug:
            wait("gpsimd", xb, 32)
            dop("gpsimd", lambda e: e.dma_start(dbg_xb, xB[:]), sdbg)
            wait("gpsimd", ac, ac.n)
            dop("gpsimd", lambda e: e.dma_start(dbg_st, states[:]), sdbg)
            wait("gpsimd", v0b, 1)
            dop("gpsimd", lambda e: e.dma_start(dbg_mv, muvar[:]), sdbg)
            dop("gpsimd", lambda e: e.dma_start(dbg_rs, rstd[:]), sdbg)
            wait("gpsimd", v5b, 1)
            dop("gpsimd", lambda e: e.dma_start(dbg_myv, muyvar[:]), sdbg)
            dop("gpsimd", lambda e: e.dma_start(dbg_rsy, rstdy[:]), sdbg)

        with nc.Block("ph1") as blk1:
            materialize(blk1, ST1)
        ph1.close()

        CUR[0] = ST2
        sb2 = lambda n, shp, dt: es.enter_context(nc.sbuf_tensor(n, shp, dt))
        ffn_fm = PH2["ffn_fm"] = sb2("ffn_fm", [128, 8 * 512], F32)
        cs_tile = PH2["cs_tile"] = sb2("cs_tile", [128, 4 * 1024], BF16)
        xrb4 = PH2["xrb4"] = sb2("xrb4", [128, 2 * DM], F32)
        out_sb = PH2["out_sb"] = sb2("out_sb", [128, 2 * DM], F32)

        PH1_FINAL = [(s_, s_.n) for s_ in (
            cld, cv0, cmm, x0, v0, sc0, v0b, x1, v1, mmT, v2, mmB, xb,
            ac, smm, cl, clB, clC, clD, cv, cs, csB, csC, csD, x2, mmC,
            v5g, v5, sc2, v5b, vsc)]
        for eng in ("sync", "vector", "scalar", "gpsimd", "tensor"):
            for s_, v_ in PH1_FINAL:
                wait(eng, s_, v_)

        if ntile > 0:
            if p3sub >= 2:
                emit_b(0)
            if p3sub >= 6:
                emit_w3(0)
        for tj in range(1, ntile):
            for _ in gen_a(tj):
                pass
            if p3sub >= 3:
                gq = {8: 0, 24: 1, 40: 2, 56: 3}
                qi = 0
                for _ in gen_l1(tj):
                    if p3sub >= 7 and qi in gq:
                        emit_g(tj - 1, gq[qi])
                    qi += 1
            if p3sub >= 2:
                emit_b(tj)
            for lay in range(2 if p3sub >= 5 else (1 if p3sub >= 4 else 0)):
                for _ in gen_h(tj, lay):
                    pass
            if p3sub >= 6:
                emit_w3(tj)
        for cloc in range(4 if (p3sub >= 7 and ntile > 0) else 0):
            emit_g(ntile - 1, cloc)
        wait("gpsimd", sout, sout.n)
        if debug:
            wait("gpsimd", sdbg, sdbg.n)

        with nc.Block("ph2") as blk2:
            materialize(blk2, ST2)

    return nc


def _tile_io(w):
    wt = np.asarray(w, dtype=np.float32).T
    v = wt.reshape(8, 128, 32, 128)
    return np.ascontiguousarray(v.transpose(2, 1, 0, 3).reshape(32, 128, 8 * 128))


def _tile_hh(w):
    wt = np.asarray(w, dtype=np.float32).T
    I, O = wt.shape
    nm = O // 128
    v = wt.reshape(I // 1024, 8, 128, nm, 128)
    return np.ascontiguousarray(v.transpose(3, 2, 0, 1, 4).reshape(nm, 128, I))


def _prep_inputs(inputs):
    f32c = lambda a: np.ascontiguousarray(np.asarray(a, dtype=np.float32))
    x = f32c(inputs["L0_attn_x"])
    common = {
        "at": f32c(np.asarray(inputs["L0_attn_A"]).T),
        "bt": f32c(np.asarray(inputs["L0_attn_B"]).T),
        "ct": f32c(np.asarray(inputs["L0_attn_C"]).T),
        "drow": f32c(np.asarray(inputs["L0_attn_D"]).reshape(1, DM)),
        "grow": f32c(np.asarray(inputs["L0_attn_gamma"]).reshape(1, DM)),
        "brow": f32c(np.asarray(inputs["L0_attn_beta"]).reshape(1, DM)),
        "w1t": _tile_io(inputs["L0_ffn_w1_w"]),
        "w2t": _tile_io(inputs["L0_ffn_w2_w"]),
        "h1t": _tile_hh(inputs["L0_ffn_h1_w"]),
        "h2t": _tile_hh(inputs["L0_ffn_h2_w"]),
        "w3t": _tile_hh(inputs["L0_ffn_w3_w"]),
        "s1pk": f32c(np.asarray(inputs["L0_ffn_w1_s"]).reshape(32, 128).T),
        "s2pk": f32c(np.asarray(inputs["L0_ffn_w2_s"]).reshape(32, 128).T),
        "h1spk": f32c(np.asarray(inputs["L0_ffn_h1_s"]).reshape(32, 128).T),
        "h2spk": f32c(np.asarray(inputs["L0_ffn_h2_s"]).reshape(32, 128).T),
        "w3spk": f32c(np.asarray(inputs["L0_ffn_w3_s"]).reshape(8, 128).T),
        "eye": np.eye(128, dtype=np.float32),
        "ones": np.ones((1, 128), dtype=np.float32),
    }
    in_maps = []
    for c in range(8):
        m = dict(common)
        m["x"] = np.ascontiguousarray(x[2 * c:2 * c + 2])
        in_maps.append(m)
    return in_maps


def kernel(**inputs):
    in_maps = _prep_inputs(inputs)
    nc = _build(debug=False)
    res = run_bass_kernel_spmd(nc, in_maps, core_ids=list(range(8)))
    return np.concatenate([r["out"] for r in res.results], axis=0)
